# version 01
# baseline (speedup 1.0000x reference)
import numpy as np
import ml_dtypes
import concourse.bass as bass
import concourse.mybir as mybir
from concourse.bass_utils import run_bass_kernel_spmd

F32 = mybir.dt.float32
BF16 = mybir.dt.bfloat16
AF = mybir.ActivationFunctionType
ALU = mybir.AluOpType
AX = mybir.AxisListType

T = 4096
D = 1024
NT = 512
NCH = T // NT
ND = D // 128

DEV_STOP_AFTER = None



class Op:
    __slots__ = ("idx", "eng", "fn", "deps", "dma", "sem", "val", "marked", "tag")

    def __init__(self, idx, eng, fn, deps, dma, tag):
        self.idx = idx
        self.eng = eng
        self.fn = fn
        self.deps = deps
        self.dma = dma
        self.marked = False
        self.sem = None
        self.val = None
        self.tag = tag


class Em:

    ENGINES = ("t", "v", "s", "g", "y")

    def __init__(self, nc):
        self.nc = nc
        self.ops = []
        self.writes = {}
        self.reads = {}
        self.tag = ""

    def _reg(self, ap):
        t = ap.tensor
        name = t.name
        space = str(ap.space)
        if "PSUM" in space:
            return (name, 0, 128)
        try:
            plo = int(ap.base_partition())
            psz = int(ap.partition_size())
        except Exception:
            plo, psz = 0, 128
        if "DRAM" in space or "Dram" in space:
            return (name, 0, 1 << 20)
        return (name, plo, plo + psz)

    def add(self, eng, fn, reads=(), writes=(), dma=False, tag=None,
            extra_deps=()):
        idx = len(self.ops)
        deps = set(extra_deps)
        rregs = [self._reg(a) for a in reads]
        wregs = [self._reg(a) for a in writes]
        for (name, lo, hi) in rregs:
            for (plo, phi, oi) in self.writes.get(name, ()):
                if plo < hi and lo < phi:
                    deps.add(oi)
        for (name, lo, hi) in wregs:
            for (plo, phi, oi) in self.writes.get(name, ()):
                if plo < hi and lo < phi:
                    deps.add(oi)
            for (plo, phi, oi) in self.reads.get(name, ()):
                if plo < hi and lo < phi:
                    deps.add(oi)
        if eng == "t" and not dma:
            deps = {d for d in deps
                    if not (self.ops[d].eng == "t" and not self.ops[d].dma)}
        op = Op(idx, eng, fn, deps, dma, tag or self.tag)
        self.ops.append(op)
        for (name, lo, hi) in wregs:
            lst = self.writes.setdefault(name, [])
            lst[:] = [e for e in lst if not (lo <= e[0] and e[1] <= hi)]
            lst.append((lo, hi, idx))
            rl = self.reads.get(name)
            if rl:
                rl[:] = [e for e in rl if not (lo <= e[0] and e[1] <= hi)]
        for (name, lo, hi) in rregs:
            self.reads.setdefault(name, []).append((lo, hi, idx))
        return idx

    def mm(self, out, lhsT, rhs, start=True, stop=True, tile_position=None):
        def f(e, out=out, lhsT=lhsT, rhs=rhs, start=start, stop=stop,
              tp=tile_position):
            return e.matmul(out, lhsT, rhs, start=start, stop=stop,
                            tile_position=tp)
        return self.add("t", f, reads=[lhsT, rhs], writes=[out])

    def trans(self, out, in_, ident):
        def f(e, out=out, in_=in_, ident=ident):
            return e.transpose(out, in_, ident)
        return self.add("t", f, reads=[in_, ident], writes=[out])

    def act(self, out, in_, func, bias=None, scale=None, accum_out=None):
        kw = {}
        if bias is not None:
            kw["bias"] = bias
        if scale is not None:
            kw["scale"] = scale
        if accum_out is not None:
            kw["accum_out"] = accum_out
        reads = [in_] + ([bias] if isinstance(bias, bass.AP) else []) \
            + ([scale] if isinstance(scale, bass.AP) else [])
        writes = [out] + ([accum_out] if accum_out is not None else [])

        def f(e, out=out, in_=in_, func=func, kw=kw):
            return e.activation(out, in_, func, **kw)
        return self.add("s", f, reads=reads, writes=writes)

    def scopy(self, out, in_):
        return self.act(out, in_, AF.Copy)

    def tt(self, out, a, b, op, eng="v"):
        def f(e, out=out, a=a, b=b, op=op):
            return e.tensor_tensor(out, a, b, op)
        return self.add(eng, f, reads=[a, b], writes=[out])

    def ts(self, out, in0, s1, s2, op0, op1=None, eng="v"):
        reads = [in0] + [x for x in (s1, s2) if isinstance(x, bass.AP)]

        def f(e, out=out, in0=in0, s1=s1, s2=s2, op0=op0, op1=op1):
            if op1 is None:
                return e.tensor_scalar(out, in0, s1, None, op0=op0)
            return e.tensor_scalar(out, in0, s1, s2, op0=op0, op1=op1)
        return self.add(eng, f, reads=reads, writes=[out])

    def stt(self, out, in0, scalar, in1, op0, op1, eng="v"):
        reads = [in0, in1] + ([scalar] if isinstance(scalar, bass.AP) else [])

        def f(e, out=out, in0=in0, scalar=scalar, in1=in1, op0=op0, op1=op1):
            return e.scalar_tensor_tensor(out, in0, scalar, in1, op0, op1)
        return self.add(eng, f, reads=reads, writes=[out])

    def vcopy(self, out, in_, eng="v"):
        def f(e, out=out, in_=in_):
            return e.tensor_copy(out, in_)
        return self.add(eng, f, reads=[in_], writes=[out])

    def reduce(self, out, in_, op, absval=None, negate=None, eng="v"):
        def f(e, out=out, in_=in_, op=op, absval=absval, negate=negate):
            return e.tensor_reduce(out, in_, axis=AX.X, op=op,
                                   apply_absolute_value=absval, negate=negate)
        return self.add(eng, f, reads=[in_], writes=[out])

    def recip(self, out, in_):
        def f(e, out=out, in_=in_):
            return e.reciprocal(out, in_)
        return self.add("v", f, reads=[in_], writes=[out])

    def scan(self, out, a, b, init, op0, op1):
        reads = [a, b] + ([init] if isinstance(init, bass.AP) else [])

        def f(e, out=out, a=a, b=b, init=init, op0=op0, op1=op1):
            return e.tensor_tensor_scan(out, a, b, init, op0=op0, op1=op1)
        return self.add("v", f, reads=reads, writes=[out])

    def select(self, out, mask, on_true, on_false):
        reads = [mask] + [x for x in (on_true, on_false)
                          if isinstance(x, bass.AP)]

        def f(e, out=out, mask=mask, t_=on_true, f_=on_false):
            return e.select(out, mask, t_, f_)
        return self.add("v", f, reads=reads, writes=[out])

    def memset(self, ap, val, eng="v"):
        def f(e, ap=ap, val=val):
            return e.memset(ap, val)
        return self.add(eng, f, reads=[], writes=[ap])

    def dma(self, q, out, in_):
        def f(e, out=out, in_=in_):
            return e.dma_start(out, in_)
        return self.add(q, f, reads=[in_], writes=[out], dma=True)

    NDMASEM = 22

    def emit(self):
        nc = self.nc
        for op in self.ops:
            for d in op.deps:
                self.ops[d].marked = True
        for op in self.ops:
            if op.dma:
                op.marked = True
        sems = {}
        for e in self.ENGINES:
            sems[e] = nc.alloc_semaphore(f"sem_{e}")
        for q in ("y", "g", "s"):
            for i in range(self.NDMASEM):
                sems[f"d{q}{i}"] = nc.alloc_semaphore(f"sem_d{q}{i}")
        counts = {k: 0 for k in sems}
        rot = {"y": 0, "g": 0, "s": 0}
        streams = {e: [] for e in self.ENGINES}
        predrain = {}
        for op in self.ops:
            if op.dma:
                q = op.eng
                key = f"d{q}{rot[q] % self.NDMASEM}"
                rot[q] += 1
                if counts[key] > 0:
                    predrain[op.idx] = (key, counts[key])
                counts[key] += 16
                op.sem = key
                op.val = counts[key]
            elif op.marked:
                counts[op.eng] += 1
                op.sem = op.eng
                op.val = counts[op.eng]
            streams[op.eng].append(op)
        self._final_counts = dict(counts)
        hw = {e: {} for e in self.ENGINES}

        def run_stream(ename, eng_obj):
            h = hw[ename]
            for op in streams[ename]:
                for d in sorted(op.deps):
                    p = self.ops[d]
                    if p.sem is None:
                        continue
                    if h.get(p.sem, 0) >= p.val:
                        continue
                    eng_obj.wait_ge(sems[p.sem], p.val)
                    h[p.sem] = p.val
                pd = predrain.get(op.idx)
                if pd is not None and h.get(pd[0], 0) < pd[1]:
                    eng_obj.wait_ge(sems[pd[0]], pd[1])
                    h[pd[0]] = pd[1]
                inst = op.fn(eng_obj)
                if op.sem is not None:
                    inst.then_inc(sems[op.sem], 16 if op.dma else 1)
            if ename == "y":
                for k, v in counts.items():
                    if k[0] == "d" and v > hw[ename].get(k, 0):
                        eng_obj.wait_ge(sems[k], v)

        with nc.Block() as block:
            @block.tensor
            def _(e):
                run_stream("t", e)

            @block.vector
            def _(e):
                run_stream("v", e)

            @block.scalar
            def _(e):
                run_stream("s", e)

            @block.gpsimd
            def _(e):
                run_stream("g", e)

            @block.sync
            def _(e):
                run_stream("y", e)

    def stats(self):
        from collections import Counter
        c = Counter((o.eng, o.dma) for o in self.ops)
        n_marked = sum(1 for o in self.ops if o.marked)
        n_waits = "?"
        return f"ops={len(self.ops)} marked={n_marked} by_engine={dict(c)}"



BIAS_REG = [
    ("b1a", 8 * 16), ("b2a", 8 * 8),
    ("b1f", 8 * 16), ("b3f", 8 * 16), ("b2f", 8 * 8),
    ("sb1f", 16), ("sb3f", 16), ("sb2", 8),
    ("gb8", 8),
    ("bga", 8),
    ("wi", 8), ("wf", 8), ("wh", 8),
    ("lnw1", 32), ("lnb1", 32),
    ("rb8", 8),
    ("bq", 8), ("bk", 8), ("bv", 8), ("bo", 8),
    ("g1", 8), ("be1", 8), ("g2", 8), ("be2", 8),
    ("b1_3", 32), ("b2_3", 8),
    ("lgam", 8), ("lb1", 4), ("lb2", 8),
    ("hgam", 8), ("hb1", 32), ("hb2", 8),
    ("cst", 4),
]
BIAS_OFF = {}
_off = 0
for _n, _c in BIAS_REG:
    BIAS_OFF[_n] = _off
    _off += _c
NB = _off


def _col_blocks(v):
    v = np.asarray(v, dtype=np.float32)
    if v.ndim == 1:
        return v.reshape(-1, 128)
    return v.reshape(v.shape[0], -1, 128).reshape(-1, 128)


def _prep_shared(inp):
    perm = np.arange(2048).reshape(1024, 2).T.reshape(-1)
    g_first = np.concatenate([np.arange(0, 2048, 2), np.arange(1, 2048, 2)])
    C = np.ascontiguousarray
    d = {}
    d["w1a"] = C(inp["L0_attn_W1"])
    d["w2a"] = C(inp["L0_attn_W2"])
    d["gw16"] = C(np.concatenate(
        [inp["L0_attn_Wg"], inp["L0_ffn_gate_w"].T], axis=1))
    d["w1f"] = C(inp["L0_ffn_w1"].transpose(0, 2, 1)[:, :, g_first])
    d["w3f"] = C(inp["L0_ffn_w3"].transpose(0, 2, 1)[:, :, g_first])
    d["w2f"] = C(inp["L0_ffn_w2"].transpose(0, 2, 1))
    d["sw1f"] = C(inp["L0_ffn_sw1"].T[:, g_first])
    d["sw3f"] = C(inp["L0_ffn_sw3"].T[:, g_first])
    d["sw2f"] = C(inp["L0_ffn_sw2"].T)
    bias = {}
    bias["b1a"] = _col_blocks(inp["L0_attn_b1"])
    bias["b2a"] = _col_blocks(inp["L0_attn_b2"])
    bias["b1f"] = _col_blocks(inp["L0_ffn_b1"][:, g_first])
    bias["b3f"] = _col_blocks(inp["L0_ffn_b3"][:, g_first])
    bias["b2f"] = _col_blocks(inp["L0_ffn_b2"])
    bias["sb1f"] = _col_blocks(inp["L0_ffn_sb1"][g_first])
    bias["sb3f"] = _col_blocks(inp["L0_ffn_sb3"][g_first])
    bias["sb2"] = _col_blocks(inp["L0_ffn_sb2"])
    bias["gb8"] = np.repeat(np.asarray(inp["L0_ffn_gate_b"], np.float32)[:, None], 128, 1)
    bias["bga"] = np.repeat(np.asarray(inp["L0_attn_bg"], np.float32)[:, None], 128, 1)
    bias["wi"] = _col_blocks(inp["L1_attn_w_i"])
    bias["wf"] = _col_blocks(inp["L1_attn_w_f"])
    bias["wh"] = _col_blocks(inp["L1_attn_w_h"])
    d["wfc"] = C(inp["L1_ffn_W_fc"].T)
    d["wproj"] = C(inp["L1_ffn_W_proj"].T)
    bias["lnw1"] = _col_blocks(inp["L1_ffn_ln_w"])
    bias["lnb1"] = _col_blocks(inp["L1_ffn_ln_b"])
    d["wq2"] = C(inp["L2_attn_wq"].T)
    d["wka"] = C(inp["L2_attn_wk_a"].T)
    d["wkb"] = C(inp["L2_attn_wk_b"].T)
    d["wva"] = C(inp["L2_attn_wv_a"].T)
    d["wvb"] = C(inp["L2_attn_wv_b"].T)
    d["wo2"] = C(inp["L2_attn_wo"].T)
    Rm = _rope_mats()
    R2 = np.zeros((128, 1024), np.float32)
    for ot in range(8):
        for hh in range(2):
            R2[hh * 64:(hh + 1) * 64, ot * 64:(ot + 1) * 64] = Rm[2 * ot + hh]
    d["ropeT2"] = np.ascontiguousarray(R2)
    d["sw1_2"] = C(inp["L2_ffn_sw1"])
    d["sw2_2"] = C(inp["L2_ffn_sw2"])
    d["sw3_2"] = C(inp["L2_ffn_sw3"])
    d["ew1"] = C(inp["L2_ffn_ew1"])
    d["ew2"] = C(inp["L2_ffn_ew2"])
    d["ew3"] = C(inp["L2_ffn_ew3"])
    d["rw8"] = C(inp["L2_ffn_rw"])
    bias["rb8"] = np.repeat(np.asarray(inp["L2_ffn_rb"], np.float32)[:, None], 128, 1)
    d["wq3"] = C(inp["L3_attn_Wq"])
    d["wk3"] = C(inp["L3_attn_Wk"])
    d["wv3"] = C(inp["L3_attn_Wv"])
    d["wo3"] = C(inp["L3_attn_Wo"])
    d["w1_3"] = C(inp["L3_attn_W1"])
    d["w2_3"] = C(inp["L3_attn_W2"])
    for nm, key in [("bq", "L3_attn_bq"), ("bk", "L3_attn_bk"), ("bv", "L3_attn_bv"),
                    ("bo", "L3_attn_bo"), ("g1", "L3_attn_g1"), ("be1", "L3_attn_be1"),
                    ("g2", "L3_attn_g2"), ("be2", "L3_attn_be2"),
                    ("b1_3", "L3_attn_b1"), ("b2_3", "L3_attn_b2"),
                    ("lgam", "L3_ffn_light_gamma"), ("lb1", "L3_ffn_light_b1"),
                    ("lb2", "L3_ffn_light_b2"), ("hgam", "L3_ffn_heavy_gamma"),
                    ("hb1", "L3_ffn_heavy_b1"), ("hb2", "L3_ffn_heavy_b2")]:
        bias[nm] = _col_blocks(inp[key])
    d["rt3"] = C(np.asarray(inp["L3_ffn_routing_token"], np.float32)[:, None])
    d["lw1"] = C(inp["L3_ffn_light_w1"])
    d["lw2"] = C(inp["L3_ffn_light_w2"])
    d["hw1"] = C(inp["L3_ffn_heavy_w1"])
    d["hw2"] = C(inp["L3_ffn_heavy_w2"])
    bias["cst"] = np.array([[v] * 128 for v in (1e-5, 1e-12, 1e-6, 1.0)],
                           np.float32)
    pack = np.zeros((NB, 128), np.float32)
    for nm, ncols in BIAS_REG:
        blk = bias[nm]
        assert blk.shape == (ncols, 128), (nm, blk.shape)
        pack[BIAS_OFF[nm]:BIAS_OFF[nm] + ncols] = blk
    d["biasv"] = C(pack.T)
    d["ident"] = np.eye(128, dtype=ml_dtypes.bfloat16)
    d["ones_c"] = np.ones((128, 1), dtype=ml_dtypes.bfloat16)
    d["ones_r"] = np.ones((1, 128), dtype=ml_dtypes.bfloat16)
    d["ident_f"] = np.eye(128, dtype=np.float32)
    d["ones_rf"] = np.ones((1, 128), dtype=np.float32)
    idx = (np.arange(32)[None, :] * 128 + np.arange(128)[:, None]).astype(np.float32)
    d["tokkey"] = np.ascontiguousarray(1.0e6 - idx)
    return d


def _rope_mats():
    inv = (1.0 / 10000.0 ** (np.arange(0, 64, 2, dtype=np.float32) / np.float32(64.0)))
    inv = inv.astype(np.float32)
    ang = (np.arange(2048, dtype=np.float32)[:, None] * inv[None, :]).astype(np.float32)
    cosf = np.cos(ang).astype(np.float32)
    sinf = np.sin(ang).astype(np.float32)
    R = np.zeros((16, 64, 64), np.float32)
    for h in range(16):
        for i in range(8):
            c, s = cosf[h, i], sinf[h, i]
            R[h, 2 * i, 2 * i] = c
            R[h, 2 * i, 2 * i + 1] = s
            R[h, 2 * i + 1, 2 * i] = -s
            R[h, 2 * i + 1, 2 * i + 1] = c
        for j in range(16, 64):
            R[h, j, j] = 1.0
    return np.ascontiguousarray(R)



class Ctx:
    def __init__(self, nc, em, W):
        self.nc, self.em, self.W = nc, em, W
        A = nc.alloc_sbuf_tensor
        self.xn = [A(f"xn{t}", [128, T], BF16) for t in range(ND)]
        self.WB = [A(f"WB{i}", [128, 2048], BF16) for i in range(16)]
        self.WC = [A(f"WC{i}", [128, 2048], BF16) for i in range(8)]
        self.hb = [A(f"hb{i}", [128, NT], BF16) for i in range(16)]
        self.ybuf = A("ybuf", [128, T // 2], F32)
        self.fs = [A(f"fs{i}", [128, NT], F32) for i in range(3)]
        self.selcw = A("selcw", [128, 512], BF16)
        self.biasv = A("biasv_sb", [128, NB], F32)
        self.gsm = [A(f"gsm{k}", [128, 32], BF16) for k in range(ND)]
        self.gtile = A("gtile", [128, 16], F32)
        self.gscr = A("gscr", [128, 48], F32)
        self.gcol = A("gcol", [128, 8], F32)
        self.rowf = A("rowf", [128, 1024], F32)
        self.rowb = A("rowb", [128, 2048], BF16)
        self.brow = [self.rowb[0:1, i * NT:(i + 1) * NT] for i in range(2)]
        self.mu_bf = self.rowb[0:1, 1024:1280]
        self.rs_bf = self.rowb[0:1, 1280:1536]
        self.r4 = self.rowf[0:1, 0:1024]
        self.bcsb = [A(f"bcsb{i}", [128, NT], BF16) for i in range(2)]
        self.ones_col = A("ones_col", [128, 1], BF16)
        self.ones_row = A("ones_row", [1, 128], BF16)
        self.id_t = A("id_t", [128, 128], BF16)
        self.id_f = A("id_f", [128, 128], F32)
        self.ones_rf = A("ones_rf_sb", [1, 128], F32)
        self.gsm_tokkey = A("tokkey_sb", [128, 32], F32)
        self.ps = [nc.alloc_psum_tensor(f"ps{i}", [128, 512], F32) for i in range(8)]
        em.dma("y", self.ones_col[:], W["ones_c"])
        em.dma("y", self.ones_row[:], W["ones_r"])
        em.dma("y", self.id_t[:], W["ident"])
        em.dma("y", self.id_f[:], W["ident_f"])
        em.dma("y", self.ones_rf[:], W["ones_rf"])
        em.dma("y", self.gsm_tokkey[:], W["tokkey"])
        em.dma("y", self.biasv[:], W["biasv"])

    def bcol(self, name, i=0):
        return self.biasv[:, BIAS_OFF[name] + i: BIAS_OFF[name] + i + 1]

    def btile(self, name, n):
        return self.biasv[:, BIAS_OFF[name]: BIAS_OFF[name] + n]

    def psbf(self, i):
        return self.ps[i][:].bitcast(BF16)


def ln_pass(cx, xsrc, acc):
    em = cx.em
    NL = 256
    for c in range(T // NL):
        cs = slice(c * NL, (c + 1) * NL)

        def xf(t):
            return cx.ybuf[:, t * NL:(t + 1) * NL]
        for t in range(ND):
            src = xsrc[t * 128:(t + 1) * 128, cs]
            em.dma("y", xf(t), src)
            em.dma("g", cx.hb[t][:, 0:NL], src)
            em.dma("s", acc[t * 128:(t + 1) * 128, cs], xf(t))
            em.act(cx.hb[8 + t][:, 0:NL], cx.hb[t][:, 0:NL], AF.Square)
        for t in range(ND):
            em.mm(cx.ps[0][0:1, 0:NL], cx.ones_col[:], cx.hb[t][:, 0:NL],
                  start=(t == 0), stop=(t == ND - 1))
        for t in range(ND):
            em.mm(cx.ps[1][0:1, 0:NL], cx.ones_col[:], cx.hb[8 + t][:, 0:NL],
                  start=(t == 0), stop=(t == ND - 1))
        mu = cx.r4[:, 0:NL]
        r0 = cx.r4[:, 256:256 + NL]
        r1 = cx.r4[:, 512:512 + NL]
        rs = cx.r4[:, 768:768 + NL]
        em.act(mu, cx.ps[0][0:1, 0:NL], AF.Copy, scale=1.0 / D)
        em.act(r0, cx.ps[1][0:1, 0:NL], AF.Copy, scale=1.0 / D)
        em.tt(r1, mu, mu, ALU.mult)
        em.tt(r0, r0, r1, ALU.subtract)
        em.act(r1, r0, AF.Ln, bias=cx.bcol("cst", 0)[0:1, :])
        em.act(rs, r1, AF.Exp, scale=-0.5)
        em.vcopy(cx.mu_bf, mu)
        em.vcopy(cx.rs_bf, rs)
        em.mm(cx.ps[2][:, 0:NL], cx.ones_row[:], cx.mu_bf)
        em.mm(cx.ps[3][:, 0:NL], cx.ones_row[:], cx.rs_bf)
        for t in range(ND):
            em.tt(xf(t), xf(t), cx.ps[2][:, 0:NL], ALU.subtract)
            em.tt(cx.xn[t][:, cs], xf(t), cx.ps[3][:, 0:NL], ALU.mult)


def bcast_col(cx, cols, bank, brow, bcsb=None, out_bf=True):
    em = cx.em
    pbf = cx.psbf(bank)
    for i, col in enumerate(cols):
        em.trans(pbf[0:1, i * 128:(i + 1) * 128], col, cx.id_t[:])
    em.act(brow[:], pbf[0:1, 0:512], AF.Copy)
    em.mm(cx.ps[bank][:], cx.ones_row[:], brow[:])
    if bcsb is not None:
        em.act(bcsb[:], cx.ps[bank][:], AF.Copy)
    return cx.ps[bank][:]


def accum_dma(cx, acc, c, half):
    em = cx.em
    src = cx.ybuf[:]
    dst = acc[half * 512:(half + 1) * 512, c * NT:(c + 1) * NT] \
        .rearrange("(a p) n -> p a n", p=128)
    s = src.rearrange("p (a n) -> p a n", n=NT)

    def f(e, dst=dst, s=s):
        return e.dma_start(dst, s, accum_op=ALU.add)
    return em.add("g", f, reads=[src], writes=[acc[:, :]], dma=True)



def l0_gates(cx):
    em = cx.em
    for k in range(ND):
        em.dma("g", cx.gsm[k][:, 0:16], cx.W["gw16"][k * 128:(k + 1) * 128, :])
    for c in range(NCH):
        for i in range(4):
            tok = slice(c * NT + i * 128, c * NT + (i + 1) * 128)
            for k in range(ND):
                em.mm(cx.ps[4][0:128, 0:16], cx.xn[k][:, tok], cx.gsm[k][:, 0:16],
                      start=(k == 0), stop=(k == ND - 1))
            g = cx.gtile
            em.act(g[:], cx.ps[4][0:128, 0:16], AF.Copy)
            col = lambda j: cx.gcol[:, j:j + 1]
            em.tt(g[:, 0:8], g[:, 0:8], cx.btile("bga", 8), ALU.add)
            em.reduce(col(0), g[:, 0:8], ALU.max, negate=True)
            ea = cx.gscr[:, 0:8]
            em.act(ea, g[:, 0:8], AF.Exp, bias=col(0))
            em.reduce(col(1), ea, ALU.add)
            em.recip(col(2), col(1))
            em.reduce(col(3), ea, ALU.max)
            em.tt(col(4), col(3), col(2), ALU.mult)
            msk = cx.gscr[:, 8:16]
            em.ts(msk, ea, col(3), None, op0=ALU.is_equal)
            em.ts(cx.selcw[:, (c * 4 + i) * 16:(c * 4 + i) * 16 + 8],
                  msk, col(4), None, op0=ALU.mult)
            em.reduce(col(0), g[:, 8:16], ALU.max, negate=True)
            ef = cx.gscr[:, 16:24]
            em.act(ef, g[:, 8:16], AF.Exp, bias=col(0))
            em.reduce(col(1), ef, ALU.add)
            em.recip(col(2), col(1))
            sc = cx.gscr[:, 24:32]
            em.ts(sc, ef, col(2), None, op0=ALU.mult)
            bi = cx.gscr[:, 32:40]
            em.tt(bi, sc, cx.btile("gb8", 8), ALU.add)
            em.reduce(col(5), bi, ALU.max)
            mk1 = cx.gscr[:, 0:8]
            em.ts(mk1, bi, col(5), None, op0=ALU.is_equal)
            bm = cx.gscr[:, 40:48]
            em.stt(bm, mk1, -1e30, bi, ALU.mult, ALU.add)
            em.reduce(col(6), bm, ALU.max)
            mk2 = cx.gscr[:, 8:16]
            em.ts(mk2, bm, col(6), None, op0=ALU.is_equal)
            em.tt(mk1, mk1, mk2, ALU.add)
            em.tt(cx.selcw[:, (c * 4 + i) * 16 + 8:(c * 4 + i) * 16 + 16],
                  mk1, sc, ALU.mult)


def l0_attn_experts(cx, acc):
    em = cx.em
    W = cx.W
    for e in range(8):
        for k in range(ND):
            em.dma("g", cx.WB[k][:], W["w1a"][e, k * 128:(k + 1) * 128, :])
        for f2 in range(8):
            em.dma("g", cx.WB[8 + f2][:, 0:1024],
                   W["w2a"][e, (2 * f2) * 128:(2 * f2 + 1) * 128, :])
            em.dma("g", cx.WB[8 + f2][:, 1024:2048],
                   W["w2a"][e, (2 * f2 + 1) * 128:(2 * f2 + 2) * 128, :])
        for c in range(NCH):
            cs = slice(c * NT, (c + 1) * NT)
            cols = [cx.selcw[:, (c * 4 + i) * 16 + e:(c * 4 + i) * 16 + e + 1]
                    for i in range(4)]
            bcast_col(cx, cols, 6, cx.brow[0], cx.bcsb[0])
            for fi in range(16):
                zb = cx.ps[4 + (fi % 2)]
                for k in range(ND):
                    em.mm(zb[:], cx.WB[k][:, fi * 128:(fi + 1) * 128],
                          cx.xn[k][:, cs], start=(k == 0), stop=(k == ND - 1))
                em.act(cx.hb[fi][:], zb[:], AF.Relu, bias=cx.bcol("b1a", e * 16 + fi))
            for di in range(8):
                yb = cx.ps[di % 4]
                for f in range(16):
                    f2, half = f // 2, f % 2
                    em.mm(yb[:], cx.WB[8 + f2][:, half * 1024 + di * 128:
                                               half * 1024 + (di + 1) * 128],
                          cx.hb[f][:], start=(f == 0), stop=(f == 15))
                em.stt(cx.ybuf[:, (di % 4) * NT:(di % 4 + 1) * NT], yb[:],
                       cx.bcol("b2a", e * 8 + di), cx.bcsb[0][:],
                       ALU.add, ALU.mult)
                if di % 4 == 3:
                    accum_dma(cx, acc, c, di // 4)


def _swiglu_phase1(cx, c, e_or_none, w1set, w3set, b1name, b3name, boff):
    em = cx.em
    cs = slice(c * NT, (c + 1) * NT)
    for fa in range(8):
        pg1, pg3, pl1, pl3 = cx.ps[4], cx.ps[5], cx.ps[6], cx.ps[7]
        for k in range(ND):
            st, sp = (k == 0), (k == ND - 1)
            em.mm(pg1[:], w1set[k][:, fa * 128:(fa + 1) * 128], cx.xn[k][:, cs], start=st, stop=sp)
        for k in range(ND):
            st, sp = (k == 0), (k == ND - 1)
            em.mm(pg3[:], w3set[k][:, fa * 128:(fa + 1) * 128], cx.xn[k][:, cs], start=st, stop=sp)
        for k in range(ND):
            st, sp = (k == 0), (k == ND - 1)
            em.mm(pl1[:], w1set[k][:, 1024 + fa * 128:1024 + (fa + 1) * 128], cx.xn[k][:, cs], start=st, stop=sp)
        for k in range(ND):
            st, sp = (k == 0), (k == ND - 1)
            em.mm(pl3[:], w3set[k][:, 1024 + fa * 128:1024 + (fa + 1) * 128], cx.xn[k][:, cs], start=st, stop=sp)
        em.act(cx.hb[8][:], pg1[:], AF.Identity, bias=cx.bcol(b1name, boff + fa))
        em.act(cx.hb[9][:], pg3[:], AF.Identity, bias=cx.bcol(b3name, boff + fa))
        em.act(cx.hb[10][:], pl1[:], AF.Identity, bias=cx.bcol(b1name, boff + 8 + fa))
        em.act(cx.hb[11][:], pl3[:], AF.Identity, bias=cx.bcol(b3name, boff + 8 + fa))
        em.tt(cx.fs[0][:], cx.hb[8][:], cx.hb[9][:], ALU.mult)
        em.tt(cx.fs[1][:], cx.hb[10][:], cx.hb[11][:], ALU.mult)
        em.ts(cx.fs[0][:], cx.fs[0][:], 7.0, None, op0=ALU.min)
        em.ts(cx.fs[1][:], cx.fs[1][:], 7.0, -7.0, op0=ALU.min, op1=ALU.max)
        em.act(cx.fs[2][:], cx.fs[0][:], AF.Sigmoid, scale=1.702)
        em.tt(cx.fs[0][:], cx.fs[0][:], cx.fs[2][:], ALU.mult)
        em.ts(cx.fs[1][:], cx.fs[1][:], 1.0, None, op0=ALU.add)
        em.tt(cx.hb[fa][:], cx.fs[0][:], cx.fs[1][:], ALU.mult)


def l0_ffn_experts(cx, acc):
    em = cx.em
    W = cx.W
    for e in range(8):
        for k in range(ND):
            em.dma("g", cx.WB[k][:], W["w1f"][e, k * 128:(k + 1) * 128, :])
            em.dma("g", cx.WB[8 + k][:], W["w3f"][e, k * 128:(k + 1) * 128, :])
            em.dma("g", cx.WC[k // 2][:, (k % 2) * 1024:(k % 2) * 1024 + 1024],
                   W["w2f"][e, k * 128:(k + 1) * 128, :])
        for c in range(NCH):
            cols = [cx.selcw[:, (c * 4 + i) * 16 + 8 + e:(c * 4 + i) * 16 + 9 + e]
                    for i in range(4)]
            bcast_col(cx, cols, 6, cx.brow[0], cx.bcsb[0])
            _swiglu_phase1(cx, c, e, cx.WB[0:8], cx.WB[8:16], "b1f", "b3f", e * 16)
            for di in range(8):
                yb = cx.ps[di % 4]
                for k in range(8):
                    em.mm(yb[:], cx.WC[k // 2][:, (k % 2) * 1024 + di * 128:
                                               (k % 2) * 1024 + (di + 1) * 128],
                          cx.hb[k][:], start=(k == 0), stop=(k == 7))
                em.stt(cx.ybuf[:, (di % 4) * NT:(di % 4 + 1) * NT], yb[:],
                       cx.bcol("b2f", e * 8 + di), cx.bcsb[0][:],
                       ALU.add, ALU.mult)
                if di % 4 == 3:
                    accum_dma(cx, acc, c, di // 4)


def l0_shared(cx, acc):
    em = cx.em
    W = cx.W
    for k in range(ND):
        em.dma("g", cx.WB[k][:], W["sw1f"][k * 128:(k + 1) * 128, :])
        em.dma("g", cx.WB[8 + k][:], W["sw3f"][k * 128:(k + 1) * 128, :])
        em.dma("g", cx.WC[k // 2][:, (k % 2) * 1024:(k % 2) * 1024 + 1024],
               W["sw2f"][k * 128:(k + 1) * 128, :])
    for c in range(NCH):
        _swiglu_phase1(cx, c, None, cx.WB[0:8], cx.WB[8:16], "sb1f", "sb3f", 0)
        for di in range(8):
            yb = cx.ps[di % 4]
            for k in range(8):
                em.mm(yb[:], cx.WC[k // 2][:, (k % 2) * 1024 + di * 128:
                                           (k % 2) * 1024 + (di + 1) * 128],
                      cx.hb[k][:], start=(k == 0), stop=(k == 7))
            em.act(cx.ybuf[:, (di % 4) * NT:(di % 4 + 1) * NT], yb[:], AF.Identity,
                   bias=cx.bcol("sb2", di))
            if di % 4 == 3:
                accum_dma(cx, acc, c, di // 4)


def emit_block0(cx, xsrc, acc):
    ln_pass(cx, xsrc, acc)
    l0_gates(cx)
    l0_attn_experts(cx, acc)
    l0_ffn_experts(cx, acc)
    l0_shared(cx, acc)



BLOCK_EMITTERS = {}


def build_program(nblocks=4):
    nc = bass.Bass("TRN2", target_bir_lowering=False)
    Wnp = _WSHAPES
    W = {}
    for name, (shape, dt) in Wnp.items():
        W[name] = nc.dram_tensor(name, list(shape), dt, kind="ExternalInput").ap()
    x_in = nc.dram_tensor("x", [D, T], F32, kind="ExternalInput").ap()
    W["hscr"] = nc.dram_tensor("hscr", [4096, T], BF16, kind="Internal").ap()
    W["thrscr"] = nc.dram_tensor("thrscr", [1, 1], F32, kind="Internal").ap()
    W["qkvscr"] = nc.dram_tensor("qkvscr", [3072, T], BF16, kind="Internal").ap()
    W["ctxscr"] = nc.dram_tensor("ctxscr", [1024, T], BF16, kind="Internal").ap()
    W["aoscr"] = nc.dram_tensor("aoscr", [1024, T], F32, kind="Internal").ap()
    W["pre2scr"] = nc.dram_tensor("pre2scr", [1024, T], F32, kind="Internal").ap()
    yout = nc.dram_tensor("yout", [D, T], F32, kind="ExternalOutput").ap()
    xs1 = nc.dram_tensor("xs1", [D, T], F32, kind="Internal").ap()
    xs2 = nc.dram_tensor("xs2", [D, T], F32, kind="Internal").ap()

    em = Em(nc)
    cx = Ctx(nc, em, W)
    srcs = [x_in, xs1, xs2, xs1]
    accs = [xs1, xs2, xs1, yout]
    accs[nblocks - 1] = yout
    emitters = [emit_block0, emit_block1, emit_block2, emit_block3][:nblocks]
    for b, fn in enumerate(emitters):
        fn(cx, srcs[b], accs[b])
    em.emit()
    return nc, em


_WSHAPES = None
_WPREPPED = None


def _prep(inputs):
    global _WSHAPES, _WPREPPED
    d = _prep_shared(inputs)
    _WPREPPED = d
    _WSHAPES = {}
    for k, v in d.items():
        dt = BF16 if v.dtype == ml_dtypes.bfloat16 else F32
        _WSHAPES[k] = (v.shape, dt)
    return d


def kernel(**inputs):
    d = _prep(inputs)
    x = np.asarray(inputs["L0_attn_x"], dtype=np.float32)
    nblocks = 4 if DEV_STOP_AFTER is None else DEV_STOP_AFTER + 1
    nc, em = build_program(nblocks=nblocks)
    in_maps = []
    for b in range(8):
        m = dict(d)
        m["x"] = np.ascontiguousarray(x[b].T)
        in_maps.append(m)
    res = run_bass_kernel_spmd(nc, in_maps, core_ids=list(range(8)))
    out = np.stack([np.asarray(r["yout"], dtype=np.float32).T
                    for r in res.results], axis=0)
    return out



def l1_attn(cx, acc):
    em = cx.em

    def yb(i):
        return cx.ybuf[:, i * NT:(i + 1) * NT]
    for t in range(ND):
        wf = cx.bcol("wf", t)
        wi = cx.bcol("wi", t)
        wh = cx.bcol("wh", t)
        for c in range(NCH):
            cs = slice(c * NT, (c + 1) * NT)
            xnc = cx.xn[t][:, cs]
            f = cx.fs[0][:]
            i_ = cx.fs[1][:]
            em.ts(f, xnc, wf, None, op0=ALU.mult)
            em.ts(i_, xnc, wi, None, op0=ALU.mult)
            em.act(yb(0), f, AF.Exp, scale=-1.0)
            em.act(yb(0), yb(0), AF.Ln, bias=cx.bcol("cst", 3))
            em.act(yb(1), i_, AF.Exp, scale=-1.0)
            em.act(yb(1), yb(1), AF.Ln, bias=cx.bcol("cst", 3))
            em.tt(f, yb(0), yb(1), ALU.subtract)
            em.act(yb(0), f, AF.Exp)
            em.ts(yb(0), yb(0), 1.0, None, op0=ALU.add)
            em.recip(cx.fs[2][:], yb(0))
            em.act(yb(0), f, AF.Exp, scale=-1.0)
            em.ts(yb(0), yb(0), 1.0, None, op0=ALU.add)
            em.recip(i_, yb(0))
            em.ts(yb(0), xnc, wh, None, op0=ALU.mult)
            em.act(yb(1), yb(0), AF.Exp, scale=-1.0)
            em.ts(yb(1), yb(1), 1.0, None, op0=ALU.add)
            em.recip(yb(2), yb(1))
            em.ts(yb(1), yb(0), 0.5, None, op0=ALU.add)
            em.ts(yb(3), yb(0), 0.0, None, op0=ALU.is_ge)
            em.tt(yb(1), yb(1), yb(2), ALU.subtract)
            em.tt(yb(1), yb(3), yb(1), ALU.mult)
            em.tt(f, yb(1), yb(2), ALU.add)
            em.tt(i_, i_, f, ALU.mult)
            init = 0.0 if c == 0 else cx.gcol[:, 0:1]
            em.scan(yb(0), cx.fs[2][:], i_, init, op0=ALU.mult, op1=ALU.add)
            em.vcopy(cx.gcol[:, 0:1], yb(0)[:, NT - 1:NT])
            dst = acc[t * 128:(t + 1) * 128, cs]

            def fdma(e, dst=dst, s=yb(0)):
                return e.dma_start(dst, s, accum_op=ALU.add)
            em.add("g", fdma, reads=[yb(0)], writes=[acc[:, :]], dma=True)


def _absmean_quant(cx, wdram, nk, dst_tiles):
    em = cx.em
    F = wdram.shape[1]
    SW = min(F, 2048)
    nh = F // SW
    colsum = cx.gcol[:, 1:2]
    first = True
    for k in range(nk):
        for h2 in range(nh):
            em.dma("y", cx.ybuf[:, 0:SW], wdram[k * 128:(k + 1) * 128,
                                                h2 * SW:(h2 + 1) * SW])
            em.reduce(cx.gcol[:, 2:3], cx.ybuf[:, 0:SW], ALU.add, absval=True)
            if first:
                em.vcopy(colsum, cx.gcol[:, 2:3])
                first = False
            else:
                em.tt(colsum, colsum, cx.gcol[:, 2:3], ALU.add)
    em.trans(cx.ps[7][0:1, 0:128], colsum, cx.id_f[:])
    em.act(cx.r4[:, 0:128], cx.ps[7][0:1, 0:128], AF.Copy)
    em.reduce(cx.r4[:, 128:129], cx.r4[:, 0:128], ALU.add)
    n = float(128 * nk * F)
    em.ts(cx.r4[:, 129:130], cx.r4[:, 128:129], 0.5 / n, 0.5e-6,
          op0=ALU.mult, op1=ALU.add)
    em.mm(cx.ps[7][:, 0:1], cx.ones_rf[:], cx.r4[:, 129:130])
    em.act(cx.gcol[:, 3:4], cx.ps[7][:, 0:1], AF.Copy)
    thr = cx.gcol[:, 3:4]
    for k in range(nk):
        for h2 in range(nh):
            em.dma("y", cx.ybuf[:, 0:SW], wdram[k * 128:(k + 1) * 128,
                                                h2 * SW:(h2 + 1) * SW])
            sgn = cx.fs[0][:]
            for q in range(SW // NT):
                qs = slice(q * NT, (q + 1) * NT)
                em.ts(sgn, cx.ybuf[:, qs], 0.0, None, op0=ALU.is_gt)
                em.ts(sgn, sgn, 2.0, -1.0, op0=ALU.mult, op1=ALU.add)
                em.act(cx.fs[1][:], cx.ybuf[:, qs], AF.Abs)
                em.ts(cx.fs[1][:], cx.fs[1][:], thr, None, op0=ALU.is_gt)
                dt, dcols = dst_tiles[k * nh + h2]
                assert dcols.stop - dcols.start == SW
                em.tt(dt[:, dcols.start + q * NT:dcols.start + (q + 1) * NT],
                      sgn, cx.fs[1][:], ALU.mult)


def l1_ffn(cx, acc):
    em = cx.em
    W = cx.W
    NL = 256
    dst = []
    for k in range(ND):
        dst.append((cx.WB[2 * k], slice(0, 2048)))
        dst.append((cx.WB[2 * k + 1], slice(0, 2048)))
    _absmean_quant(cx, W["wfc"], ND, dst)

    def wfc_tile(k, hi):
        tens = cx.WB[2 * k + (hi // 16)]
        return tens[:, (hi % 16) * 128:((hi % 16) + 1) * 128]
    for c in range(T // NL):
        cs = slice(c * NL, (c + 1) * NL)
        for hi in range(32):
            zb = cx.ps[4 + hi % 2]
            for k in range(ND):
                em.mm(zb[:, 0:NL], wfc_tile(k, hi), cx.xn[k][:, cs],
                      start=(k == 0), stop=(k == ND - 1))
            em.act(cx.hb[hi // 2][:, (hi % 2) * NL:(hi % 2 + 1) * NL],
                   zb[:, 0:NL], AF.Gelu)
        for hi in range(32):
            hsl = cx.hb[hi // 2][:, (hi % 2) * NL:(hi % 2 + 1) * NL]
            em.mm(cx.ps[0][0:1, 0:NL], cx.ones_col[:], hsl,
                  start=(hi == 0), stop=(hi == 31))
        for hi in range(32):
            hsl = cx.hb[hi // 2][:, (hi % 2) * NL:(hi % 2 + 1) * NL]
            sq = cx.bcsb[hi % 2]
            em.act(sq[:, 0:NL], hsl, AF.Square)
            em.mm(cx.ps[1][0:1, 0:NL], cx.ones_col[:], sq[:, 0:NL],
                  start=(hi == 0), stop=(hi == 31))
        mu = cx.r4[:, 0:NL]
        r0 = cx.r4[:, 256:256 + NL]
        r1 = cx.r4[:, 512:512 + NL]
        rs = cx.r4[:, 768:768 + NL]
        em.act(mu, cx.ps[0][0:1, 0:NL], AF.Copy, scale=1.0 / 4096)
        em.act(r0, cx.ps[1][0:1, 0:NL], AF.Copy, scale=1.0 / 4096)
        em.tt(r1, mu, mu, ALU.mult)
        em.tt(r0, r0, r1, ALU.subtract)
        em.act(r1, r0, AF.Ln, bias=cx.bcol("cst", 0)[0:1, :])
        em.act(rs, r1, AF.Exp, scale=-0.5)
        em.vcopy(cx.mu_bf, mu)
        em.vcopy(cx.rs_bf, rs)
        em.mm(cx.ps[2][:, 0:NL], cx.ones_row[:], cx.mu_bf)
        em.mm(cx.ps[3][:, 0:NL], cx.ones_row[:], cx.rs_bf)
        for hi in range(32):
            hsl = cx.hb[hi // 2][:, (hi % 2) * NL:(hi % 2 + 1) * NL]
            em.tt(cx.fs[0][:, 0:NL], hsl, cx.ps[2][:, 0:NL], ALU.subtract)
            em.tt(cx.fs[0][:, 0:NL], cx.fs[0][:, 0:NL], cx.ps[3][:, 0:NL],
                  ALU.mult)
            em.ts(cx.fs[1][:, 0:NL].bitcast(BF16)[:, 0:NL], cx.fs[0][:, 0:NL],
                  cx.bcol("lnw1", hi), cx.bcol("lnb1", hi),
                  op0=ALU.mult, op1=ALU.add)
            em.dma("s", W["hscr"][hi * 128:(hi + 1) * 128, cs],
                   cx.fs[1][:, 0:NL].bitcast(BF16)[:, 0:NL])
    dst = []
    for k in range(32):
        tens = cx.WB[k // 2]
        dst.append((tens, slice((k % 2) * 1024, (k % 2) * 1024 + 1024)))
    _absmean_quant(cx, W["wproj"], 32, dst)

    def wproj_tile(k, di):
        tens = cx.WB[k // 2]
        return tens[:, (k % 2) * 1024 + di * 128:(k % 2) * 1024 + (di + 1) * 128]
    for c in range(T // NL):
        cs = slice(c * NL, (c + 1) * NL)
        for k in range(32):
            em.dma("g", cx.hb[k // 2][:, (k % 2) * NL:(k % 2 + 1) * NL],
                   W["hscr"][k * 128:(k + 1) * 128, cs])
        for di in range(ND):
            zb = cx.ps[di % 4]
            for k in range(32):
                em.mm(zb[:, 0:NL], wproj_tile(k, di),
                      cx.hb[k // 2][:, (k % 2) * NL:(k % 2 + 1) * NL],
                      start=(k == 0), stop=(k == 31))
            em.act(cx.ybuf[:, (di % 4) * NL:(di % 4 + 1) * NL], zb[:, 0:NL],
                   AF.Copy)
            if di % 4 == 3:
                half = di // 4
                src = cx.ybuf[:, 0:4 * NL]
                dst2 = acc[half * 512:(half + 1) * 512, cs]                     .rearrange("(a p) n -> p a n", p=128)
                s2 = src.rearrange("p (a n) -> p a n", n=NL)

                def fdma(e, dst2=dst2, s2=s2):
                    return e.dma_start(dst2, s2, accum_op=ALU.add)
                em.add("g", fdma, reads=[src], writes=[acc[:, :]], dma=True)


def emit_block1(cx, xsrc, acc):
    ln_pass(cx, xsrc, acc)
    l1_attn(cx, acc)
    l1_ffn(cx, acc)



def l2_attn(cx, acc):
    em = cx.em
    W = cx.W
    qkv = W["qkvscr"]
    ctxs = W["ctxscr"]
    for k in range(ND):
        em.dma("g", cx.WB[k // 2][:, (k % 2) * 1024:(k % 2) * 1024 + 1024],
               W["wq2"][k * 128:(k + 1) * 128, :])
        em.dma("g", cx.WB[4][:, k * 256:(k + 1) * 256],
               W["wka"][k * 128:(k + 1) * 128, :])
        em.dma("g", cx.WB[5][:, k * 256:(k + 1) * 256],
               W["wva"][k * 128:(k + 1) * 128, :])
        em.dma("g", cx.WB[8 + k // 2][:, (k % 2) * 1024:(k % 2) * 1024 + 1024],
               W["wo2"][k * 128:(k + 1) * 128, :])
    for k in range(2):
        em.dma("g", cx.WB[6][:, k * 1024:(k + 1) * 1024],
               W["wkb"][k * 128:(k + 1) * 128, :])
        em.dma("g", cx.WB[7][:, k * 1024:(k + 1) * 1024],
               W["wvb"][k * 128:(k + 1) * 128, :])
    em.dma("g", cx.WB[12][:, 0:1024], W["ropeT2"])

    def _rope_store(c, base_row):
        cs = slice(c * NT, (c + 1) * NT)
        for ot in range(8):
            zb = cx.ps[6]
            for hh in range(2):
                em.mm(zb[hh * 64:(hh + 1) * 64, :],
                      cx.WB[12][hh * 64:(hh + 1) * 64, ot * 64:(ot + 1) * 64],
                      cx.hb[ot][hh * 64:(hh + 1) * 64, :])
            em.act(cx.hb[8][:], zb[:], AF.Copy)
            em.dma("s", W["qkvscr"][base_row + ot * 128:base_row + (ot + 1) * 128, cs],
                   cx.hb[8][:])

    for c in range(NCH):
        cs = slice(c * NT, (c + 1) * NT)
        for ot in range(8):
            zb = cx.ps[4 + ot % 2]
            for k in range(ND):
                em.mm(zb[:], cx.WB[k // 2][:, (k % 2) * 1024 + ot * 128:
                                           (k % 2) * 1024 + (ot + 1) * 128],
                      cx.xn[k][:, cs], start=(k == 0), stop=(k == ND - 1))
            em.act(cx.hb[ot][:], zb[:], AF.Copy)
        _rope_store(c, 0)
        for lt in range(2):
            zb = cx.ps[4 + lt]
            for k in range(ND):
                em.mm(zb[:], cx.WB[4][:, k * 256 + lt * 128:k * 256 + (lt + 1) * 128],
                      cx.xn[k][:, cs], start=(k == 0), stop=(k == ND - 1))
            em.act(cx.hb[10 + lt][:], zb[:], AF.Copy)
        for ot in range(8):
            zb = cx.ps[4 + ot % 2]
            for lt in range(2):
                em.mm(zb[:], cx.WB[6][:, lt * 1024 + ot * 128:
                                      lt * 1024 + (ot + 1) * 128],
                      cx.hb[10 + lt][:], start=(lt == 0), stop=(lt == 1))
            em.act(cx.hb[ot][:], zb[:], AF.Copy)
        _rope_store(c, 1024)
        for lt in range(2):
            zb = cx.ps[4 + lt]
            for k in range(ND):
                em.mm(zb[:], cx.WB[5][:, k * 256 + lt * 128:k * 256 + (lt + 1) * 128],
                      cx.xn[k][:, cs], start=(k == 0), stop=(k == ND - 1))
            em.act(cx.hb[10 + lt][:], zb[:], AF.Copy)
        for ot in range(8):
            zb = cx.ps[4 + ot % 2]
            for lt in range(2):
                em.mm(zb[:], cx.WB[7][:, lt * 1024 + ot * 128:
                                      lt * 1024 + (ot + 1) * 128],
                      cx.hb[10 + lt][:], start=(lt == 0), stop=(lt == 1))
            em.act(cx.hb[8][:], zb[:], AF.Copy)
            em.dma("s", qkv[2048 + ot * 128:2048 + (ot + 1) * 128, cs], cx.hb[8][:])

    for h in range(16):
        for half in range(2):
            hs = slice(half * 2048, (half + 1) * 2048)
            em.dma("g", cx.WB[13][half * 64:(half + 1) * 64, :],
                   qkv[1024 + h * 64:1024 + (h + 1) * 64, hs])
            em.dma("g", cx.WB[14][half * 64:(half + 1) * 64, :],
                   qkv[2048 + h * 64:2048 + (h + 1) * 64, hs])
            em.dma("g", cx.WB[15][half * 64:(half + 1) * 64, :],
                   qkv[h * 64:(h + 1) * 64, hs])
            em.dma("g", cx.WC[0][(1 - half) * 64:(2 - half) * 64, :],
                   qkv[h * 64:(h + 1) * 64, hs])
        def vtile(j, w=65):
            if j < 31:
                return cx.WB[12][:, j * 66:j * 66 + w]
            return cx.bcsb[1][:, 0:w]
        em.memset(cx.WB[12][:], 1.0, eng="v")
        em.memset(cx.bcsb[1][:, 0:66], 1.0, eng="v")
        for j in range(32):
            b_j = (j // 16) * 64
            vsl = cx.WB[14][b_j:b_j + 64, (j % 16) * 128:(j % 16 + 1) * 128]
            em.trans(cx.psbf(7)[0:128, 0:64], vsl,
                     cx.id_t[b_j:b_j + 64, b_j:b_j + 64])
            em.act(vtile(j, 64), cx.psbf(7)[0:128, 0:64], AF.Copy)
        for qc in range(NCH):
            for j in range(32):
                b_j = (j // 16) * 64
                qsrc = cx.WB[15] if (qc // 4) * 64 == b_j else cx.WC[0]
                qrhs = qsrc[b_j:b_j + 64, (qc % 4) * 512:(qc % 4 + 1) * 512]
                sb = cx.ps[4 + j % 2]
                ktile = cx.WB[13][b_j:b_j + 64,
                                  (j % 16) * 128:(j % 16 + 1) * 128]
                em.mm(sb[:], ktile, qrhs)
                em.act(cx.hb[j % 4][:], sb[:], AF.Exp, scale=0.125)
                em.mm(cx.ps[0][0:65, :], vtile(j),
                      cx.hb[j % 4][:], start=(j == 0), stop=(j == 31))
            zr = cx.r4[:, 0:NT]
            em.recip(zr, cx.ps[0][64:65, :])
            em.vcopy(cx.brow[0], zr)
            em.mm(cx.ps[2][:], cx.ones_row[:], cx.brow[0])
            em.act(cx.fs[0][0:64, :], cx.ps[0][0:64, :], AF.Copy)
            em.tt(cx.bcsb[0][0:64, :], cx.fs[0][0:64, :], cx.ps[2][0:64, :],
                  ALU.mult)
            em.dma("s", ctxs[h * 64:(h + 1) * 64, qc * NT:(qc + 1) * NT],
                   cx.bcsb[0][0:64, :])

    for c in range(NCH):
        cs = slice(c * NT, (c + 1) * NT)
        for k in range(ND):
            em.dma("g", cx.hb[k][:], ctxs[k * 128:(k + 1) * 128, cs])
        for di in range(8):
            yb = cx.ps[di % 4]
            for k in range(ND):
                em.mm(yb[:], cx.WB[8 + k // 2][:, (k % 2) * 1024 + di * 128:
                                               (k % 2) * 1024 + (di + 1) * 128],
                      cx.hb[k][:], start=(k == 0), stop=(k == ND - 1))
            em.act(cx.ybuf[:, (di % 4) * NT:(di % 4 + 1) * NT], yb[:], AF.Copy)
            if di % 4 == 3:
                accum_dma(cx, acc, c, di // 4)


def l2_gates(cx):
    em = cx.em
    for k in range(ND):
        em.dma("g", cx.gsm[k][:, 0:8], cx.W["rw8"][k * 128:(k + 1) * 128, :])
    for c in range(NCH):
        for i in range(4):
            tok = slice(c * NT + i * 128, c * NT + (i + 1) * 128)
            for k in range(ND):
                em.mm(cx.ps[4][0:128, 0:8], cx.xn[k][:, tok], cx.gsm[k][:, 0:8],
                      start=(k == 0), stop=(k == ND - 1))
            g = cx.gtile
            em.act(g[:, 0:8], cx.ps[4][0:128, 0:8], AF.Copy)
            col = lambda j: cx.gcol[:, j:j + 1]
            em.tt(g[:, 0:8], g[:, 0:8], cx.btile("rb8", 8), ALU.add)
            em.reduce(col(0), g[:, 0:8], ALU.max, negate=True)
            ef = cx.gscr[:, 16:24]
            em.act(ef, g[:, 0:8], AF.Exp, bias=col(0))
            em.reduce(col(1), ef, ALU.add)
            em.recip(col(2), col(1))
            sc = cx.gscr[:, 24:32]
            em.ts(sc, ef, col(2), None, op0=ALU.mult)
            em.reduce(col(5), sc, ALU.max)
            mk1 = cx.gscr[:, 0:8]
            em.ts(mk1, sc, col(5), None, op0=ALU.is_equal)
            bm = cx.gscr[:, 40:48]
            em.stt(bm, mk1, -1e30, sc, ALU.mult, ALU.add)
            em.reduce(col(6), bm, ALU.max)
            mk2 = cx.gscr[:, 8:16]
            em.ts(mk2, bm, col(6), None, op0=ALU.is_equal)
            em.tt(mk1, mk1, mk2, ALU.add)
            em.tt(mk1, mk1, sc, ALU.mult)
            em.tt(col(5), col(5), col(6), ALU.add)
            em.recip(col(4), col(5))
            em.ts(cx.selcw[:, (c * 4 + i) * 16 + 8:(c * 4 + i) * 16 + 16],
                  mk1, col(4), None, op0=ALU.mult)


def l2_ffn_experts(cx, acc):
    em = cx.em
    W = cx.W
    for e in range(8):
        for k in range(ND):
            em.dma("g", cx.WB[k][:], W["ew1"][e, k * 128:(k + 1) * 128, :])
            em.dma("g", cx.WB[8 + k][:], W["ew2"][e, k * 128:(k + 1) * 128, :])
        for k in range(16):
            em.dma("g", cx.WC[k // 2][:, (k % 2) * 1024:(k % 2) * 1024 + 1024],
                   W["ew3"][e, k * 128:(k + 1) * 128, :])
        for c in range(NCH):
            cs = slice(c * NT, (c + 1) * NT)
            cols = [cx.selcw[:, (c * 4 + i) * 16 + 8 + e:(c * 4 + i) * 16 + 9 + e]
                    for i in range(4)]
            bcast_col(cx, cols, 6, cx.brow[0], cx.bcsb[0])
            for fi in range(16):
                pa, pb = cx.ps[4], cx.ps[5]
                for k in range(ND):
                    em.mm(pa[:], cx.WB[k][:, fi * 128:(fi + 1) * 128],
                          cx.xn[k][:, cs], start=(k == 0), stop=(k == ND - 1))
                for k in range(ND):
                    em.mm(pb[:], cx.WB[8 + k][:, fi * 128:(fi + 1) * 128],
                          cx.xn[k][:, cs], start=(k == 0), stop=(k == ND - 1))
                em.act(cx.bcsb[1][:], pa[:], AF.Silu)
                em.tt(cx.hb[fi][:], pb[:], cx.bcsb[1][:], ALU.mult)
            for di in range(8):
                yb = cx.ps[di % 4]
                for f in range(16):
                    em.mm(yb[:], cx.WC[f // 2][:, (f % 2) * 1024 + di * 128:
                                               (f % 2) * 1024 + (di + 1) * 128],
                          cx.hb[f][:], start=(f == 0), stop=(f == 15))
                em.tt(cx.ybuf[:, (di % 4) * NT:(di % 4 + 1) * NT], yb[:],
                      cx.bcsb[0][:], ALU.mult)
                if di % 4 == 3:
                    accum_dma(cx, acc, c, di // 4)


def l2_shared(cx, acc):
    em = cx.em
    W = cx.W
    for k in range(ND):
        em.dma("g", cx.WB[k][:], W["sw1_2"][k * 128:(k + 1) * 128, :])
        em.dma("g", cx.WB[8 + k][:], W["sw2_2"][k * 128:(k + 1) * 128, :])
    for k in range(16):
        em.dma("g", cx.WC[k // 2][:, (k % 2) * 1024:(k % 2) * 1024 + 1024],
               W["sw3_2"][k * 128:(k + 1) * 128, :])
    for c in range(NCH):
        cs = slice(c * NT, (c + 1) * NT)
        for fi in range(16):
            pa, pb = cx.ps[4], cx.ps[5]
            for k in range(ND):
                em.mm(pa[:], cx.WB[k][:, fi * 128:(fi + 1) * 128],
                      cx.xn[k][:, cs], start=(k == 0), stop=(k == ND - 1))
            for k in range(ND):
                em.mm(pb[:], cx.WB[8 + k][:, fi * 128:(fi + 1) * 128],
                      cx.xn[k][:, cs], start=(k == 0), stop=(k == ND - 1))
            em.act(cx.bcsb[1][:], pa[:], AF.Silu)
            em.tt(cx.hb[fi][:], pb[:], cx.bcsb[1][:], ALU.mult)
        for di in range(8):
            yb = cx.ps[di % 4]
            for f in range(16):
                em.mm(yb[:], cx.WC[f // 2][:, (f % 2) * 1024 + di * 128:
                                           (f % 2) * 1024 + (di + 1) * 128],
                      cx.hb[f][:], start=(f == 0), stop=(f == 15))
            em.act(cx.ybuf[:, (di % 4) * NT:(di % 4 + 1) * NT], yb[:], AF.Copy)
            if di % 4 == 3:
                accum_dma(cx, acc, c, di // 4)


def emit_block2(cx, xsrc, acc):
    ln_pass(cx, xsrc, acc)
    l2_gates(cx)
    l2_attn(cx, acc)
    l2_ffn_experts(cx, acc)
    l2_shared(cx, acc)




def l3_attn(cx, acc):
    em = cx.em
    W = cx.W
    qn_scr = W["qkvscr"]
    aoscr = W["aoscr"]
    pre2 = W["pre2scr"]
    for k in range(ND):
        em.dma("g", cx.WB[k // 2][:, (k % 2) * 1024:(k % 2) * 1024 + 1024],
               W["wq3"][k * 128:(k + 1) * 128, :])
        em.dma("g", cx.WB[4 + k // 2][:, (k % 2) * 1024:(k % 2) * 1024 + 1024],
               W["wk3"][k * 128:(k + 1) * 128, :])
        em.dma("g", cx.WB[8 + k // 2][:, (k % 2) * 1024:(k % 2) * 1024 + 1024],
               W["wv3"][k * 128:(k + 1) * 128, :])
        em.dma("g", cx.WC[k // 2][:, (k % 2) * 1024:(k % 2) * 1024 + 1024],
               W["wo3"][k * 128:(k + 1) * 128, :])
    kvsb = cx.WB[12]

    def proj(wbase, bname, ot, cs, zb):
        for k in range(ND):
            em.mm(zb[:], cx.WB[wbase + k // 2][:, (k % 2) * 1024 + ot * 128:
                                               (k % 2) * 1024 + (ot + 1) * 128],
                  cx.xn[k][:, cs], start=(k == 0), stop=(k == ND - 1))

    first_kv = [True, True]
    for c in range(NCH):
        cs = slice(c * NT, (c + 1) * NT)
        for ot in range(8):
            zb = cx.ps[4 + ot % 2]
            proj(0, "bq", ot, cs, zb)
            bq = cx.bcol("bq", ot)
            em.ts(cx.fs[0][:], zb[:], bq, 0.0, op0=ALU.add, op1=ALU.min)
            em.act(cx.fs[1][:], cx.fs[0][:], AF.Exp)
            em.ts(cx.fs[2][:], zb[:], bq, 0.0, op0=ALU.add, op1=ALU.max)
            em.stt(cx.hb[ot][:], cx.fs[2][:], -1.0, cx.fs[1][:],
                   ALU.add, ALU.add)
        for ot in range(8):
            em.act(cx.bcsb[1][:], cx.hb[ot][:], AF.Square)
            for hh in range(2):
                b = hh * 64
                em.mm(cx.ps[6][0:1, :],
                      cx.ones_col[0:64] if b == 0 else cx.ones_col[64:128],
                      cx.bcsb[1][b:b + 64, :])
                r0 = cx.r4[:, 0:NT]
                em.act(r0, cx.ps[6][0:1, :], AF.Ln)
                em.act(r0, r0, AF.Exp, scale=-0.5)
                em.vcopy(cx.brow[0], r0)
                em.mm(cx.ps[6][:], cx.ones_row[:], cx.brow[0])
                em.tt(cx.hb[ot][b:b + 64, :], cx.hb[ot][b:b + 64, :],
                      cx.ps[6][b:b + 64, :], ALU.mult)
            em.dma("s", qn_scr[ot * 128:(ot + 1) * 128, cs], cx.hb[ot][:])
        for ot in range(8):
            zb = cx.ps[4 + ot % 2]
            proj(4, "bk", ot, cs, zb)
            bk = cx.bcol("bk", ot)
            em.ts(cx.fs[0][:], zb[:], bk, 0.0, op0=ALU.add, op1=ALU.min)
            em.act(cx.fs[1][:], cx.fs[0][:], AF.Exp)
            em.ts(cx.fs[2][:], zb[:], bk, 0.0, op0=ALU.add, op1=ALU.max)
            em.stt(cx.hb[ot][:], cx.fs[2][:], -1.0, cx.fs[1][:],
                   ALU.add, ALU.add)
        for ot in range(8):
            zb = cx.ps[4 + ot % 2]
            proj(8, "bv", ot, cs, zb)
            em.act(cx.hb[8 + ot][:], zb[:], AF.Identity,
                   bias=cx.bcol("bv", ot))
        for h in range(16):
            b = (h % 2) * 64
            ident = cx.id_t[b:b + 64, b:b + 64]
            for i in range(4):
                tsl = slice(i * 128, (i + 1) * 128)
                em.trans(cx.psbf(7)[0:128, 0:64],
                         cx.hb[h // 2][b:b + 64, tsl], ident)
                em.act(cx.fs[0][:, 0:64], cx.psbf(7)[0:128, 0:64], AF.Copy)
                em.trans(cx.psbf(7)[0:128, 64:128],
                         cx.hb[8 + h // 2][b:b + 64, tsl], ident)
                em.act(cx.bcsb[1][:, 64:128], cx.psbf(7)[0:128, 64:128],
                       AF.Copy)
                em.act(cx.fs[1][:, 0:64], cx.fs[0][:, 0:64], AF.Square)
                em.reduce(cx.gcol[:, 4:5], cx.fs[1][:, 0:64], ALU.add)
                em.act(cx.gcol[:, 5:6], cx.gcol[:, 4:5], AF.Ln)
                em.act(cx.gcol[:, 5:6], cx.gcol[:, 5:6], AF.Exp, scale=-0.5)
                em.ts(cx.bcsb[1][:, 0:64], cx.fs[0][:, 0:64],
                      cx.gcol[:, 5:6], None, op0=ALU.mult)
                bank = cx.ps[h // 8]
                em.mm(bank[0:64, (h % 8) * 64:(h % 8) * 64 + 64],
                      cx.bcsb[1][:, 0:64], cx.bcsb[1][:, 64:128],
                      start=first_kv[h // 8], stop=False,
                      tile_position=None)
                first_kv[h // 8] = False
    for h in range(16):
        src = cx.ps[h // 8][0:64, (h % 8) * 64:(h % 8) * 64 + 64]
        em.act(kvsb[0:64, h * 64:(h + 1) * 64], src, AF.Copy)
        em.act(kvsb[64:128, h * 64:(h + 1) * 64], src, AF.Copy)

    NL = 256
    for c in range(T // NL):
        cs = slice(c * NL, (c + 1) * NL)
        for t in range(ND):
            em.dma("g", cx.hb[t][:, 0:NL], qn_scr[t * 128:(t + 1) * 128, cs])
        for h in range(16):
            b = (h % 2) * 64
            em.mm(cx.ps[4 + (h // 2) % 2][b:b + 64, 0:NL],
                  kvsb[b:b + 64, h * 64:(h + 1) * 64],
                  cx.hb[h // 2][b:b + 64, 0:NL])
            if h % 2 == 1:
                em.act(cx.hb[8 + h // 2][:, 0:NL],
                       cx.ps[4 + (h // 2) % 2][:, 0:NL], AF.Copy,
                       scale=0.125)
        for di in range(8):
            yb = cx.ps[di % 4]
            for k in range(ND):
                em.mm(yb[:, 0:NL], cx.WC[k // 2][:, (k % 2) * 1024 + di * 128:
                                                 (k % 2) * 1024 + (di + 1) * 128],
                      cx.hb[8 + k][:, 0:NL], start=(k == 0), stop=(k == ND - 1))
            em.stt(cx.ybuf[:, di * NL:(di + 1) * NL], yb[:, 0:NL],
                   cx.bcol("bo", di), cx.xn[di][:, cs], ALU.add, ALU.add)
        for t in range(ND):
            em.vcopy(cx.hb[t][:, 0:NL], cx.ybuf[:, t * NL:(t + 1) * NL])
            em.mm(cx.ps[4][0:1, 0:NL], cx.ones_col[:], cx.hb[t][:, 0:NL],
                  start=(t == 0), stop=(t == ND - 1))
        for t in range(ND):
            em.act(cx.bcsb[1][:, 0:NL], cx.hb[t][:, 0:NL], AF.Square)
            em.mm(cx.ps[5][0:1, 0:NL], cx.ones_col[:], cx.bcsb[1][:, 0:NL],
                  start=(t == 0), stop=(t == ND - 1))
        mu = cx.r4[:, 0:NL]
        r0 = cx.r4[:, 256:256 + NL]
        r1 = cx.r4[:, 512:512 + NL]
        rs = cx.r4[:, 768:768 + NL]
        em.act(mu, cx.ps[4][0:1, 0:NL], AF.Copy, scale=1.0 / D)
        em.act(r0, cx.ps[5][0:1, 0:NL], AF.Copy, scale=1.0 / D)
        em.tt(r1, mu, mu, ALU.mult)
        em.tt(r0, r0, r1, ALU.subtract)
        em.act(r1, r0, AF.Ln, bias=cx.bcol("cst", 1)[0:1, :])
        em.act(rs, r1, AF.Exp, scale=-0.5)
        em.vcopy(cx.mu_bf, mu)
        em.vcopy(cx.rs_bf, rs)
        em.mm(cx.ps[2][:, 0:NL], cx.ones_row[:], cx.mu_bf)
        em.mm(cx.ps[3][:, 0:NL], cx.ones_row[:], cx.rs_bf)
        for t in range(ND):
            sl = cx.ybuf[:, t * NL:(t + 1) * NL]
            em.tt(sl, sl, cx.ps[2][:, 0:NL], ALU.subtract)
            em.tt(sl, sl, cx.ps[3][:, 0:NL], ALU.mult)
            em.ts(cx.hb[t][:, 0:NL], sl, cx.bcol("g1", t), cx.bcol("be1", t),
                  op0=ALU.mult, op1=ALU.add)
            em.ts(sl, sl, cx.bcol("g1", t), cx.bcol("be1", t),
                  op0=ALU.mult, op1=ALU.add)
            em.dma("s", qn_scr[1024 + t * 128:1024 + (t + 1) * 128, cs],
                   cx.hb[t][:, 0:NL])
            em.dma("s", aoscr[t * 128:(t + 1) * 128, cs], sl)


def l3_b2(cx, W, qn_scr, pre2):
    em = cx.em
    for grp in range(4):
        fo = grp * 1024
        for k in range(ND):
            em.dma("g", cx.WB[k][:, 0:1024],
                   W["w1_3"][k * 128:(k + 1) * 128, fo:fo + 1024])
        for k in range(8):
            em.dma("g", cx.WC[k // 2][:, (k % 2) * 1024:(k % 2) * 1024 + 1024],
                   W["w2_3"][fo + k * 128:fo + (k + 1) * 128, :])
        for c in range(NCH):
            cs = slice(c * NT, (c + 1) * NT)
            for t in range(ND):
                em.dma("g", cx.hb[t][:],
                       qn_scr[1024 + t * 128:1024 + (t + 1) * 128, cs])
            for fi in range(8):
                zb = cx.ps[4 + fi % 2]
                for k in range(ND):
                    em.mm(zb[:], cx.WB[k][:, fi * 128:(fi + 1) * 128],
                          cx.hb[k][:], start=(k == 0), stop=(k == ND - 1))
                em.act(cx.hb[8 + fi][:], zb[:], AF.Gelu,
                       bias=cx.bcol("b1_3", grp * 8 + fi))
            for di in range(8):
                yb = cx.ps[di % 4]
                for k in range(8):
                    em.mm(yb[:], cx.WC[k // 2][:, (k % 2) * 1024 + di * 128:
                                               (k % 2) * 1024 + (di + 1) * 128],
                          cx.hb[8 + k][:], start=(k == 0), stop=(k == 7))
                if grp == 0:
                    em.dma("y", cx.fs[0][:], W["aoscr"][di * 128:(di + 1) * 128, cs])
                    em.stt(cx.ybuf[:, (di % 4) * NT:(di % 4 + 1) * NT], yb[:],
                           cx.bcol("b2_3", di), cx.fs[0][:], ALU.add, ALU.add)
                else:
                    em.act(cx.ybuf[:, (di % 4) * NT:(di % 4 + 1) * NT], yb[:],
                           AF.Copy)
                if di % 4 == 3:
                    src = cx.ybuf[:]
                    hdst = pre2[(di // 4) * 512:(di // 4 + 1) * 512, cs] \
                        .rearrange("(a p) n -> p a n", p=128)
                    s2 = src.rearrange("p (a n) -> p a n", n=NT)
                    if grp == 0:
                        def fdma(e, hdst=hdst, s2=s2):
                            return e.dma_start(hdst, s2)
                    else:
                        def fdma(e, hdst=hdst, s2=s2):
                            return e.dma_start(hdst, s2, accum_op=ALU.add)
                    em.add("g", fdma, reads=[src], writes=[pre2[:, :]], dma=True)


def l3_ln2(cx, W, pre2, acc):
    em = cx.em
    NL = 256
    for c in range(T // NL):
        cs = slice(c * NL, (c + 1) * NL)
        for t in range(ND):
            em.dma("y", cx.ybuf[:, t * NL:(t + 1) * NL],
                   pre2[t * 128:(t + 1) * 128, cs])
            em.dma("g", cx.hb[t][:, 0:NL], pre2[t * 128:(t + 1) * 128, cs])
            em.act(cx.hb[8 + t][:, 0:NL], cx.hb[t][:, 0:NL], AF.Square)
        for t in range(ND):
            em.mm(cx.ps[0][0:1, 0:NL], cx.ones_col[:], cx.hb[t][:, 0:NL],
                  start=(t == 0), stop=(t == ND - 1))
        for t in range(ND):
            em.mm(cx.ps[1][0:1, 0:NL], cx.ones_col[:], cx.hb[8 + t][:, 0:NL],
                  start=(t == 0), stop=(t == ND - 1))
        mu = cx.r4[:, 0:NL]
        r0 = cx.r4[:, 256:256 + NL]
        r1 = cx.r4[:, 512:512 + NL]
        rs = cx.r4[:, 768:768 + NL]
        em.act(mu, cx.ps[0][0:1, 0:NL], AF.Copy, scale=1.0 / D)
        em.act(r0, cx.ps[1][0:1, 0:NL], AF.Copy, scale=1.0 / D)
        em.tt(r1, mu, mu, ALU.mult)
        em.tt(r0, r0, r1, ALU.subtract)
        em.act(r1, r0, AF.Ln, bias=cx.bcol("cst", 1)[0:1, :])
        em.act(rs, r1, AF.Exp, scale=-0.5)
        em.vcopy(cx.mu_bf, mu)
        em.vcopy(cx.rs_bf, rs)
        em.mm(cx.ps[2][:, 0:NL], cx.ones_row[:], cx.mu_bf)
        em.mm(cx.ps[3][:, 0:NL], cx.ones_row[:], cx.rs_bf)
        for t in range(ND):
            sl = cx.ybuf[:, t * NL:(t + 1) * NL]
            em.tt(sl, sl, cx.ps[2][:, 0:NL], ALU.subtract)
            em.tt(sl, sl, cx.ps[3][:, 0:NL], ALU.mult)
            em.ts(sl, sl, cx.bcol("g2", t), cx.bcol("be2", t),
                  op0=ALU.mult, op1=ALU.add)
            dst = acc[t * 128:(t + 1) * 128, cs]

            def fdma(e, dst=dst, s=sl):
                return e.dma_start(dst, s, accum_op=ALU.add)
            em.add("g", fdma, reads=[sl], writes=[acc[:, :]], dma=True)


def l3_ffn(cx, acc):
    em = cx.em
    W = cx.W
    for k in range(ND):
        em.dma("g", cx.gsm[k][:, 16:17], W["rt3"][k * 128:(k + 1) * 128, :])
    for i in range(32):
        tok = slice(i * 128, (i + 1) * 128)
        for k in range(ND):
            em.mm(cx.ps[4][0:128, 0:1], cx.xn[k][:, tok], cx.gsm[k][:, 16:17],
                  start=(k == 0), stop=(k == ND - 1))
        em.act(cx.gscr[:, i:i + 1], cx.ps[4][0:128, 0:1], AF.Copy)
    s32 = cx.gscr[:, 0:32]
    LOGK = float(np.log(np.float32(1152.0), dtype=np.float32))
    A1 = float(np.float32(LOGK - np.log(np.float32(4096.0), dtype=np.float32)))
    a_ = cx.r4[:, 0:1]
    nega = cx.r4[:, 1:2]
    M = cx.r4[:, 2:3]
    S_ = cx.r4[:, 3:4]
    t1 = cx.r4[:, 4:5]
    t2 = cx.r4[:, 5:6]
    rowbuf = cx.r4[:, 8:136]
    em.memset(a_, A1)

    def bcast1(srcrow, dstcol):
        em.mm(cx.ps[7][:, 0:1], cx.ones_rf[:], srcrow)
        em.act(dstcol, cx.ps[7][:, 0:1], AF.Copy)

    def cross_reduce(tile, out, op):
        em.reduce(cx.gcol[:, 7:8], tile, op)
        em.trans(cx.ps[7][0:1, 0:128], cx.gcol[:, 7:8], cx.id_f[:])
        em.act(rowbuf, cx.ps[7][0:1, 0:128], AF.Copy)
        em.reduce(out, rowbuf, op)

    mtile = cx.fs[0][:, 0:32]
    etile = cx.fs[0][:, 32:64]
    for it in range(49):
        em.ts(nega, a_, -1.0, None, op0=ALU.mult)
        bcast1(nega, cx.gcol[:, 6:7])
        em.ts(mtile, s32, cx.gcol[:, 6:7], None, op0=ALU.min)
        cross_reduce(mtile, M, ALU.max)
        em.ts(t1, M, -1.0, None, op0=ALU.mult)
        bcast1(t1, cx.gcol[:, 5:6])
        em.act(etile, mtile, AF.Exp, bias=cx.gcol[:, 5:6])
        cross_reduce(etile, S_, ALU.add)
        em.act(t2, S_, AF.Ln)
        em.ts(t1, M, -1.0, LOGK, op0=ALU.mult, op1=ALU.add)
        em.tt(a_, t1, t2, ALU.subtract)
    em.ts(nega, a_, -1.0, None, op0=ALU.mult)
    bcast1(nega, cx.gcol[:, 6:7])
    em.ts(mtile, s32, cx.gcol[:, 6:7], None, op0=ALU.is_ge)
    key = cx.fs[0][:, 64:96]
    em.tt(etile, cx.gsm_tokkey[:], s32, ALU.subtract)
    em.tt(etile, etile, mtile, ALU.mult)
    em.tt(key, s32, etile, ALU.add)
    lo = cx.r4[:, 0:1]
    hi = cx.r4[:, 1:2]
    mid = cx.r4[:, 2:3]
    cnt = cx.r4[:, 3:4]
    cond = cx.r4[:, 4:5]
    tmp = cx.r4[:, 5:6]
    em.memset(lo, -1.0e6)
    em.memset(hi, 2.0e6)
    for it in range(48):
        em.tt(mid, lo, hi, ALU.add)
        em.ts(mid, mid, 0.5, None, op0=ALU.mult)
        bcast1(mid, cx.gcol[:, 6:7])
        em.ts(etile, key, cx.gcol[:, 6:7], None, op0=ALU.is_ge)
        cross_reduce(etile, cnt, ALU.add)
        em.ts(cond, cnt, 1024.0, None, op0=ALU.is_ge)
        em.tt(tmp, mid, lo, ALU.subtract)
        em.tt(tmp, tmp, cond, ALU.mult)
        em.tt(lo, lo, tmp, ALU.add)
        em.tt(tmp, hi, mid, ALU.subtract)
        em.ts(cond, cond, -1.0, 1.0, op0=ALU.mult, op1=ALU.add)
        em.tt(tmp, tmp, cond, ALU.mult)
        em.tt(hi, hi, tmp, ALU.subtract)
    bcast1(lo, cx.gcol[:, 6:7])
    em.ts(cx.selcw[:, 480:512], key, cx.gcol[:, 6:7], None, op0=ALU.is_ge)

    for k in range(ND):
        em.dma("g", cx.WC[k // 4][:, (k % 4) * 512:(k % 4) * 512 + 512],
               W["lw1"][k * 128:(k + 1) * 128, :])
    for k in range(4):
        em.dma("g", cx.WC[2 + k // 2][:, (k % 2) * 1024:(k % 2) * 1024 + 1024],
               W["lw2"][k * 128:(k + 1) * 128, :])

    def rms_tiles(cs, gname, out_tiles):
        for t in range(ND):
            em.act(cx.bcsb[1][:], cx.xn[t][:, cs], AF.Square)
            em.mm(cx.ps[6][0:1, :], cx.ones_col[:], cx.bcsb[1][:],
                  start=(t == 0), stop=(t == ND - 1))
        r0 = cx.r4[:, 0:NT]
        em.act(r0, cx.ps[6][0:1, :], AF.Ln)
        em.act(r0, r0, AF.Exp, scale=-0.5)
        em.ts(r0, r0, 32.0, None, op0=ALU.mult)
        em.vcopy(cx.brow[0], r0)
        em.mm(cx.ps[6][:], cx.ones_row[:], cx.brow[0])
        for t in range(ND):
            em.tt(cx.fs[0][:], cx.xn[t][:, cs], cx.ps[6][:], ALU.mult)
            em.ts(out_tiles[t][:], cx.fs[0][:], cx.bcol(gname, t), None,
                  op0=ALU.mult)

    for c in range(NCH):
        cs = slice(c * NT, (c + 1) * NT)
        rms_tiles(cs, "lgam", cx.hb[0:8])
        for fi in range(4):
            zb = cx.ps[4 + fi % 2]
            for k in range(ND):
                em.mm(zb[:], cx.WC[k // 4][:, (k % 4) * 512 + fi * 128:
                                           (k % 4) * 512 + (fi + 1) * 128],
                      cx.hb[k][:], start=(k == 0), stop=(k == ND - 1))
            em.act(cx.hb[8 + fi][:], zb[:], AF.Gelu, bias=cx.bcol("lb1", fi))
        for di in range(8):
            yb = cx.ps[di % 4]
            for k in range(4):
                em.mm(yb[:], cx.WC[2 + k // 2][:, (k % 2) * 1024 + di * 128:
                                               (k % 2) * 1024 + (di + 1) * 128],
                      cx.hb[8 + k][:], start=(k == 0), stop=(k == 3))
            em.act(cx.ybuf[:, (di % 4) * NT:(di % 4 + 1) * NT], yb[:],
                   AF.Identity, bias=cx.bcol("lb2", di))
            if di % 4 == 3:
                accum_dma(cx, acc, c, di // 4)

    for half in range(2):
        fo = half * 2048
        for k in range(ND):
            em.dma("g", cx.WB[k][:], W["hw1"][k * 128:(k + 1) * 128,
                                              fo:fo + 2048])
        for k in range(16):
            em.dma("g", cx.WC[k // 2][:, (k % 2) * 1024:(k % 2) * 1024 + 1024],
                   W["hw2"][fo + k * 128:fo + (k + 1) * 128, :])
        for c in range(NCH):
            cs = slice(c * NT, (c + 1) * NT)
            cols = [cx.selcw[:, 480 + c * 4 + i:481 + c * 4 + i]
                    for i in range(4)]
            bcast_col(cx, cols, 6, cx.brow[0], cx.bcsb[0])
            rms_tiles(cs, "hgam", cx.hb[0:8])
            for fi in range(16):
                zb = cx.ps[4 + fi % 2]
                for k in range(ND):
                    em.mm(zb[:], cx.WB[k][:, fi * 128:(fi + 1) * 128],
                          cx.hb[k][:], start=(k == 0), stop=(k == ND - 1))
                em.act(cx.hb[8 + fi % 8][:], zb[:], AF.Gelu,
                       bias=cx.bcol("hb1", half * 16 + fi))
                if fi % 8 == 7:
                    sub = fi // 8
                    for di in range(8):
                        yb = cx.ps[di % 4]
                        for k in range(8):
                            kk = sub * 8 + k
                            em.mm(yb[:], cx.WC[kk // 2][:, (kk % 2) * 1024 + di * 128:
                                                        (kk % 2) * 1024 + (di + 1) * 128],
                                  cx.hb[8 + k][:], start=(k == 0),
                                  stop=(k == 7))
                        if half == 0 and sub == 0:
                            em.stt(cx.fs[0][:], yb[:], cx.bcol("hb2", di),
                                   cx.bcsb[0][:], ALU.add, ALU.mult)
                        else:
                            em.tt(cx.fs[0][:], yb[:], cx.bcsb[0][:], ALU.mult)
                        em.vcopy(cx.ybuf[:, (di % 4) * NT:(di % 4 + 1) * NT],
                                 cx.fs[0][:])
                        if di % 4 == 3:
                            accum_dma(cx, acc, c, di // 4)


def emit_block3(cx, xsrc, acc):
    ln_pass(cx, xsrc, acc)
    l3_attn(cx, acc)
    l3_b2(cx, cx.W, cx.W["qkvscr"], cx.W["pre2scr"])
    l3_ln2(cx, cx.W, cx.W["pre2scr"], acc)
    l3_ffn(cx, acc)



# version 07 (final)
# speedup vs baseline: 1.0367x; 1.0367x over previous
import os
import numpy as np
import ml_dtypes
import concourse.bass as bass
import concourse.mybir as mybir
from concourse.bass_utils import run_bass_kernel_spmd

F32 = mybir.dt.float32
BF16 = mybir.dt.bfloat16
AF = mybir.ActivationFunctionType
ALU = mybir.AluOpType
AX = mybir.AxisListType

T = 4096
D = 1024
NT = 512
NCH = T // NT
ND = D // 128

DEV_STOP_AFTER = None



class Op:
    __slots__ = ("idx", "eng", "fn", "deps", "dma", "sem", "val", "marked", "tag")

    def __init__(self, idx, eng, fn, deps, dma, tag):
        self.idx = idx
        self.eng = eng
        self.fn = fn
        self.deps = deps
        self.dma = dma
        self.marked = False
        self.sem = None
        self.val = None
        self.tag = tag


class Em:

    ENGINES = ("t", "v", "s", "g", "y")

    def __init__(self, nc):
        self.nc = nc
        self.ops = []
        self.writes = {}
        self.reads = {}
        self.tag = ""

    def _reg(self, ap):
        t = ap.tensor
        name = t.name
        space = str(ap.space)
        if "PSUM" in space:
            try:
                step_p = ap.ap[0][0]
                it = 2 if ap.dtype == BF16 else 4
                flo = (ap.offset % step_p) * it if step_p else ap.offset * it
                fhi = flo + ap.free_nbytes()
                return (name, flo // 2048, (fhi + 2047) // 2048)
            except Exception:
                return (name, 0, 1 << 20)
        try:
            plo = int(ap.base_partition())
            psz = int(ap.partition_size())
        except Exception:
            plo, psz = 0, 128
        if "DRAM" in space or "Dram" in space:
            return (name, 0, 1 << 20)
        return (name, plo, plo + psz)

    def add(self, eng, fn, reads=(), writes=(), dma=False, tag=None,
            extra_deps=()):
        idx = len(self.ops)
        deps = set(extra_deps)
        rregs = [self._reg(a) for a in reads]
        wregs = [self._reg(a) for a in writes]
        for (name, lo, hi) in rregs:
            for (plo, phi, oi) in self.writes.get(name, ()):
                if plo < hi and lo < phi:
                    deps.add(oi)
        for (name, lo, hi) in wregs:
            for (plo, phi, oi) in self.writes.get(name, ()):
                if plo < hi and lo < phi:
                    deps.add(oi)
            for (plo, phi, oi) in self.reads.get(name, ()):
                if plo < hi and lo < phi:
                    deps.add(oi)
        if eng == "t" and not dma:
            deps = {d for d in deps
                    if not (self.ops[d].eng == "t" and not self.ops[d].dma)}
        op = Op(idx, eng, fn, deps, dma, tag or self.tag)
        self.ops.append(op)
        for (name, lo, hi) in wregs:
            lst = self.writes.setdefault(name, [])
            lst[:] = [e for e in lst if not (lo <= e[0] and e[1] <= hi)]
            lst.append((lo, hi, idx))
            rl = self.reads.get(name)
            if rl:
                rl[:] = [e for e in rl if not (lo <= e[0] and e[1] <= hi)]
        for (name, lo, hi) in rregs:
            self.reads.setdefault(name, []).append((lo, hi, idx))
        return idx

    def mm(self, out, lhsT, rhs, start=True, stop=True, tile_position=None):
        def f(e, out=out, lhsT=lhsT, rhs=rhs, start=start, stop=stop,
              tp=tile_position):
            return e.matmul(out, lhsT, rhs, start=start, stop=stop,
                            tile_position=tp)
        return self.add("t", f, reads=[lhsT, rhs], writes=[out])

    def trans(self, out, in_, ident):
        def f(e, out=out, in_=in_, ident=ident):
            return e.transpose(out, in_, ident)
        return self.add("t", f, reads=[in_, ident], writes=[out])

    def act(self, out, in_, func, bias=None, scale=None, accum_out=None):
        kw = {}
        if bias is not None:
            kw["bias"] = bias
        if scale is not None:
            kw["scale"] = scale
        if accum_out is not None:
            kw["accum_out"] = accum_out
        reads = [in_] + ([bias] if isinstance(bias, bass.AP) else []) \
            + ([scale] if isinstance(scale, bass.AP) else [])
        writes = [out] + ([accum_out] if accum_out is not None else [])

        def f(e, out=out, in_=in_, func=func, kw=kw):
            return e.activation(out, in_, func, **kw)
        return self.add("s", f, reads=reads, writes=writes)

    def scopy(self, out, in_):
        return self.act(out, in_, AF.Copy)

    def tt(self, out, a, b, op, eng="v"):
        def f(e, out=out, a=a, b=b, op=op):
            return e.tensor_tensor(out, a, b, op)
        return self.add(eng, f, reads=[a, b], writes=[out])

    def ts(self, out, in0, s1, s2, op0, op1=None, eng="v"):
        reads = [in0] + [x for x in (s1, s2) if isinstance(x, bass.AP)]

        def f(e, out=out, in0=in0, s1=s1, s2=s2, op0=op0, op1=op1):
            if op1 is None:
                return e.tensor_scalar(out, in0, s1, None, op0=op0)
            return e.tensor_scalar(out, in0, s1, s2, op0=op0, op1=op1)
        return self.add(eng, f, reads=reads, writes=[out])

    def stt(self, out, in0, scalar, in1, op0, op1, eng="v"):
        reads = [in0, in1] + ([scalar] if isinstance(scalar, bass.AP) else [])

        def f(e, out=out, in0=in0, scalar=scalar, in1=in1, op0=op0, op1=op1):
            return e.scalar_tensor_tensor(out, in0, scalar, in1, op0, op1)
        return self.add(eng, f, reads=reads, writes=[out])

    def vcopy(self, out, in_, eng="v"):
        def f(e, out=out, in_=in_):
            return e.tensor_copy(out, in_)
        return self.add(eng, f, reads=[in_], writes=[out])

    def reduce(self, out, in_, op, absval=None, negate=None, eng="v"):
        def f(e, out=out, in_=in_, op=op, absval=absval, negate=negate):
            return e.tensor_reduce(out, in_, axis=AX.X, op=op,
                                   apply_absolute_value=absval, negate=negate)
        return self.add(eng, f, reads=[in_], writes=[out])

    def recip(self, out, in_):
        def f(e, out=out, in_=in_):
            return e.reciprocal(out, in_)
        return self.add("v", f, reads=[in_], writes=[out])

    def scan(self, out, a, b, init, op0, op1):
        reads = [a, b] + ([init] if isinstance(init, bass.AP) else [])

        def f(e, out=out, a=a, b=b, init=init, op0=op0, op1=op1):
            return e.tensor_tensor_scan(out, a, b, init, op0=op0, op1=op1)
        return self.add("v", f, reads=reads, writes=[out])

    def select(self, out, mask, on_true, on_false):
        reads = [mask] + [x for x in (on_true, on_false)
                          if isinstance(x, bass.AP)]

        def f(e, out=out, mask=mask, t_=on_true, f_=on_false):
            return e.select(out, mask, t_, f_)
        return self.add("v", f, reads=reads, writes=[out])

    def memset(self, ap, val, eng="v"):
        def f(e, ap=ap, val=val):
            return e.memset(ap, val)
        return self.add(eng, f, reads=[], writes=[ap])

    def dma(self, q, out, in_):
        def f(e, out=out, in_=in_):
            return e.dma_start(out, in_)
        return self.add(q, f, reads=[in_], writes=[out], dma=True)

    NDMASEM = 22

    def emit(self):
        nc = self.nc
        for op in self.ops:
            for d in op.deps:
                self.ops[d].marked = True
        for op in self.ops:
            if op.dma:
                op.marked = True
        sems = {}
        for e in self.ENGINES:
            sems[e] = nc.alloc_semaphore(f"sem_{e}")
        for q in ("y", "g", "s"):
            for i in range(self.NDMASEM):
                sems[f"d{q}{i}"] = nc.alloc_semaphore(f"sem_d{q}{i}")
        counts = {k: 0 for k in sems}
        rot = {"y": 0, "g": 0, "s": 0}
        streams = {e: [] for e in self.ENGINES}
        predrain = {}
        for op in self.ops:
            if op.dma:
                q = op.eng
                key = f"d{q}{rot[q] % self.NDMASEM}"
                rot[q] += 1
                if counts[key] > 0:
                    predrain[op.idx] = (key, counts[key])
                counts[key] += 16
                op.sem = key
                op.val = counts[key]
            elif op.marked:
                counts[op.eng] += 1
                op.sem = op.eng
                op.val = counts[op.eng]
            streams[op.eng].append(op)
        self._final_counts = dict(counts)
        hw = {e: {} for e in self.ENGINES}

        def run_stream(ename, eng_obj):
            h = hw[ename]
            for op in streams[ename]:
                for d in sorted(op.deps):
                    p = self.ops[d]
                    if p.sem is None:
                        continue
                    if h.get(p.sem, 0) >= p.val:
                        continue
                    eng_obj.wait_ge(sems[p.sem], p.val)
                    h[p.sem] = p.val
                pd = predrain.get(op.idx)
                if pd is not None and h.get(pd[0], 0) < pd[1]:
                    eng_obj.wait_ge(sems[pd[0]], pd[1])
                    h[pd[0]] = pd[1]
                inst = op.fn(eng_obj)
                if op.sem is not None:
                    inst.then_inc(sems[op.sem], 16 if op.dma else 1)
            if ename == "y":
                for k, v in counts.items():
                    if k[0] == "d" and v > hw[ename].get(k, 0):
                        eng_obj.wait_ge(sems[k], v)

        with nc.Block() as block:
            @block.tensor
            def _(e):
                run_stream("t", e)

            @block.vector
            def _(e):
                run_stream("v", e)

            @block.scalar
            def _(e):
                run_stream("s", e)

            @block.gpsimd
            def _(e):
                run_stream("g", e)

            @block.sync
            def _(e):
                run_stream("y", e)

    def stats(self):
        from collections import Counter
        c = Counter((o.eng, o.dma) for o in self.ops)
        n_marked = sum(1 for o in self.ops if o.marked)
        n_waits = "?"
        return f"ops={len(self.ops)} marked={n_marked} by_engine={dict(c)}"



BIAS_REG = [
    ("b1a", 8 * 16), ("b2a", 8 * 8),
    ("b1f", 8 * 16), ("b3f", 8 * 16), ("b2f", 8 * 8),
    ("sb1f", 16), ("sb3f", 16), ("sb2", 8),
    ("gb8", 8),
    ("bga", 8),
    ("wi", 8), ("wf", 8), ("wh", 8),
    ("lnw1", 32), ("lnb1", 32),
    ("rb8", 8),
    ("bq", 8), ("bk", 8), ("bv", 8), ("bo", 8),
    ("g1", 8), ("be1", 8), ("g2", 8), ("be2", 8),
    ("b1_3", 32), ("b2_3", 8),
    ("lgam", 8), ("lb1", 4), ("lb2", 8),
    ("hgam", 8), ("hb1", 32), ("hb2", 8),
    ("cst", 4),
]
BIAS_OFF = {}
_off = 0
for _n, _c in BIAS_REG:
    BIAS_OFF[_n] = _off
    _off += _c
NB = _off


def _col_blocks(v):
    v = np.asarray(v, dtype=np.float32)
    if v.ndim == 1:
        return v.reshape(-1, 128)
    return v.reshape(v.shape[0], -1, 128).reshape(-1, 128)


def _prep_shared(inp):
    perm = np.arange(2048).reshape(1024, 2).T.reshape(-1)
    g_first = np.concatenate([np.arange(0, 2048, 2), np.arange(1, 2048, 2)])
    C = np.ascontiguousarray
    d = {}
    d["w1a"] = C(inp["L0_attn_W1"])
    d["w2a"] = C(inp["L0_attn_W2"])
    d["gw16"] = C(np.concatenate(
        [inp["L0_attn_Wg"], inp["L0_ffn_gate_w"].T], axis=1))
    d["w1f"] = C(inp["L0_ffn_w1"].transpose(0, 2, 1)[:, :, g_first])
    d["w3f"] = C(inp["L0_ffn_w3"].transpose(0, 2, 1)[:, :, g_first])
    d["w2f"] = C(inp["L0_ffn_w2"].transpose(0, 2, 1))
    d["sw1f"] = C(inp["L0_ffn_sw1"].T[:, g_first])
    d["sw3f"] = C(inp["L0_ffn_sw3"].T[:, g_first])
    d["sw2f"] = C(inp["L0_ffn_sw2"].T)
    bias = {}
    bias["b1a"] = _col_blocks(inp["L0_attn_b1"])
    bias["b2a"] = _col_blocks(inp["L0_attn_b2"])
    bias["b1f"] = _col_blocks(inp["L0_ffn_b1"][:, g_first])
    bias["b3f"] = _col_blocks(inp["L0_ffn_b3"][:, g_first])
    bias["b2f"] = _col_blocks(inp["L0_ffn_b2"])
    bias["sb1f"] = _col_blocks(inp["L0_ffn_sb1"][g_first])
    bias["sb3f"] = _col_blocks(inp["L0_ffn_sb3"][g_first])
    bias["sb2"] = _col_blocks(inp["L0_ffn_sb2"])
    bias["gb8"] = np.repeat(np.asarray(inp["L0_ffn_gate_b"], np.float32)[:, None], 128, 1)
    bias["bga"] = np.repeat(np.asarray(inp["L0_attn_bg"], np.float32)[:, None], 128, 1)
    bias["wi"] = _col_blocks(inp["L1_attn_w_i"])
    bias["wf"] = _col_blocks(inp["L1_attn_w_f"])
    bias["wh"] = _col_blocks(inp["L1_attn_w_h"])
    d["wfc"] = C(inp["L1_ffn_W_fc"].T)
    d["wproj"] = C(inp["L1_ffn_W_proj"].T)
    bias["lnw1"] = _col_blocks(inp["L1_ffn_ln_w"])
    bias["lnb1"] = _col_blocks(inp["L1_ffn_ln_b"])
    d["wq2"] = C(inp["L2_attn_wq"].T)
    d["wka"] = C(inp["L2_attn_wk_a"].T)
    d["wkb"] = C(inp["L2_attn_wk_b"].T)
    d["wva"] = C(inp["L2_attn_wv_a"].T)
    d["wvb"] = C(inp["L2_attn_wv_b"].T)
    d["wo2"] = C(inp["L2_attn_wo"].T)
    Rm = _rope_mats()
    R2 = np.zeros((128, 1024), np.float32)
    for ot in range(8):
        for hh in range(2):
            R2[hh * 64:(hh + 1) * 64, ot * 64:(ot + 1) * 64] = Rm[2 * ot + hh]
    d["ropeT2"] = np.ascontiguousarray(R2)
    d["sw1_2"] = C(inp["L2_ffn_sw1"])
    d["sw2_2"] = C(inp["L2_ffn_sw2"])
    d["sw3_2"] = C(inp["L2_ffn_sw3"])
    d["ew1"] = C(inp["L2_ffn_ew1"])
    d["ew2"] = C(inp["L2_ffn_ew2"])
    d["ew3"] = C(inp["L2_ffn_ew3"])
    d["rw8"] = C(inp["L2_ffn_rw"])
    bias["rb8"] = np.repeat(np.asarray(inp["L2_ffn_rb"], np.float32)[:, None], 128, 1)
    d["wq3"] = C(inp["L3_attn_Wq"])
    d["wk3"] = C(inp["L3_attn_Wk"])
    d["wv3"] = C(inp["L3_attn_Wv"])
    d["wo3"] = C(inp["L3_attn_Wo"])
    d["w1_3"] = C(inp["L3_attn_W1"])
    d["w2_3"] = C(inp["L3_attn_W2"])
    for nm, key in [("bq", "L3_attn_bq"), ("bk", "L3_attn_bk"), ("bv", "L3_attn_bv"),
                    ("bo", "L3_attn_bo"), ("g1", "L3_attn_g1"), ("be1", "L3_attn_be1"),
                    ("g2", "L3_attn_g2"), ("be2", "L3_attn_be2"),
                    ("b1_3", "L3_attn_b1"), ("b2_3", "L3_attn_b2"),
                    ("lgam", "L3_ffn_light_gamma"), ("lb1", "L3_ffn_light_b1"),
                    ("lb2", "L3_ffn_light_b2"), ("hgam", "L3_ffn_heavy_gamma"),
                    ("hb1", "L3_ffn_heavy_b1"), ("hb2", "L3_ffn_heavy_b2")]:
        bias[nm] = _col_blocks(inp[key])
    d["rt3"] = C(np.asarray(inp["L3_ffn_routing_token"], np.float32)[:, None])
    d["lw1"] = C(inp["L3_ffn_light_w1"])
    d["lw2"] = C(inp["L3_ffn_light_w2"])
    d["hw1"] = C(inp["L3_ffn_heavy_w1"])
    d["hw2"] = C(inp["L3_ffn_heavy_w2"])
    bias["cst"] = np.array([[v] * 128 for v in (1e-5, 1e-12, 1e-6, 1.0)],
                           np.float32)
    pack = np.zeros((NB, 128), np.float32)
    for nm, ncols in BIAS_REG:
        blk = bias[nm]
        assert blk.shape == (ncols, 128), (nm, blk.shape)
        pack[BIAS_OFF[nm]:BIAS_OFF[nm] + ncols] = blk
    d["biasv"] = C(pack.T)
    d["ident"] = np.eye(128, dtype=ml_dtypes.bfloat16)
    d["ones_c"] = np.ones((128, 1), dtype=ml_dtypes.bfloat16)
    d["ones_r"] = np.ones((1, 128), dtype=ml_dtypes.bfloat16)
    d["ident_f"] = np.eye(128, dtype=np.float32)
    d["ones_rf"] = np.ones((1, 128), dtype=np.float32)
    idx = (np.arange(32)[None, :] * 128 + np.arange(128)[:, None]).astype(np.float32)
    d["tokkey"] = np.ascontiguousarray(1.0e6 - idx)
    return d


def _rope_mats():
    inv = (1.0 / 10000.0 ** (np.arange(0, 64, 2, dtype=np.float32) / np.float32(64.0)))
    inv = inv.astype(np.float32)
    ang = (np.arange(2048, dtype=np.float32)[:, None] * inv[None, :]).astype(np.float32)
    cosf = np.cos(ang).astype(np.float32)
    sinf = np.sin(ang).astype(np.float32)
    R = np.zeros((16, 64, 64), np.float32)
    for h in range(16):
        for i in range(8):
            c, s = cosf[h, i], sinf[h, i]
            R[h, 2 * i, 2 * i] = c
            R[h, 2 * i, 2 * i + 1] = s
            R[h, 2 * i + 1, 2 * i] = -s
            R[h, 2 * i + 1, 2 * i + 1] = c
        for j in range(16, 64):
            R[h, j, j] = 1.0
    return np.ascontiguousarray(R)



class Ctx:
    def __init__(self, nc, em, W):
        self.nc, self.em, self.W = nc, em, W
        A = nc.alloc_sbuf_tensor
        self.xn = [A(f"xn{t}", [128, T], BF16) for t in range(ND)]
        self.WB = [A(f"WB{i}", [128, 2048], BF16) for i in range(16)]
        self.WC = [A(f"WC{i}", [128, 2048], BF16) for i in range(8)]
        self.hb = [A(f"hb{i}", [128, NT], BF16) for i in range(16)]
        self.ybuf = A("ybuf", [128, T // 2], F32)
        self.fsc = [A(f"fsc{i}", [128, NT], F32) for i in range(2)]
        self.fs = [A("fs0", [128, NT], F32), A("fs1", [128, NT], F32),
                   self.fsc[0]]
        self.selcw = A("selcw", [128, 512], BF16)
        self.biasv = A("biasv_sb", [128, NB], F32)
        self.gsm = [A(f"gsm{k}", [128, 32], BF16) for k in range(ND)]
        self.gtile = A("gtile", [128, 16], F32)
        self.gscr = A("gscr", [128, 48], F32)
        self.gcol = A("gcol", [128, 8], F32)
        self.rowf = A("rowf", [128, 1024], F32)
        self.rowb = A("rowb", [128, 1536], BF16)
        self.brow = [self.rowb[0:1, i * NT:(i + 1) * NT] for i in range(2)]
        self.mu_bf = self.rowb[0:1, 1024:1280]
        self.rs_bf = self.rowb[0:1, 1280:1536]
        self.r4 = self.rowf[0:1, 0:1024]
        self.bcsb = [A(f"bcsb{i}", [128, NT], BF16) for i in range(2)]
        self.ones_col = A("ones_col", [128, 1], BF16)
        self.ones_row = A("ones_row", [1, 128], BF16)
        self.id_t = A("id_t", [128, 128], BF16)
        self.id_f = A("id_f", [128, 128], F32)
        self.ones_rf = A("ones_rf_sb", [1, 128], F32)
        self.gsm_tokkey = A("tokkey_sb", [128, 32], F32)
        _p = [nc.alloc_psum_tensor(f"ps{i}", [128, 512], F32) for i in range(4)]
        self._psS = nc.alloc_psum_tensor("psS", [128, 1024], F32)
        self._psS2 = nc.alloc_psum_tensor("psS2", [128, 1024], F32)
        self.ps = [t[:] for t in _p] + \
            [self._psS[:, 0:512], self._psS[:, 512:1024],
             self._psS2[:, 0:512], self._psS2[:, 512:1024]]
        self.psS = [self._psS[:], self._psS2[:]]
        em.dma("y", self.ones_col[:], W["ones_c"])
        em.dma("y", self.ones_row[:], W["ones_r"])
        em.dma("y", self.id_t[:], W["ident"])
        em.dma("y", self.id_f[:], W["ident_f"])
        em.dma("y", self.ones_rf[:], W["ones_rf"])
        em.dma("y", self.gsm_tokkey[:], W["tokkey"])
        em.dma("y", self.biasv[:], W["biasv"])

    def bcol(self, name, i=0):
        return self.biasv[:, BIAS_OFF[name] + i: BIAS_OFF[name] + i + 1]

    def btile(self, name, n):
        return self.biasv[:, BIAS_OFF[name]: BIAS_OFF[name] + n]

    def psbf(self, i):
        return self.ps[i].bitcast(BF16)


def ln_pass(cx, xsrc, acc):
    em = cx.em
    NL = 256
    for c in range(T // NL):
        cs = slice(c * NL, (c + 1) * NL)

        def xf(t):
            return cx.ybuf[:, t * NL:(t + 1) * NL]
        for t in range(ND):
            src = xsrc[t * 128:(t + 1) * 128, cs]
            em.dma("y", xf(t), src)
            em.dma("g", cx.hb[t][:, 0:NL], src)
            em.dma("s", acc[t * 128:(t + 1) * 128, cs], xf(t))
            em.act(cx.hb[8 + t][:, 0:NL], cx.hb[t][:, 0:NL], AF.Square)
        for t in range(ND):
            em.mm(cx.ps[0][0:1, 0:NL], cx.ones_col[:], cx.hb[t][:, 0:NL],
                  start=(t == 0), stop=(t == ND - 1))
        for t in range(ND):
            em.mm(cx.ps[1][0:1, 0:NL], cx.ones_col[:], cx.hb[8 + t][:, 0:NL],
                  start=(t == 0), stop=(t == ND - 1))
        mu = cx.r4[:, 0:NL]
        r0 = cx.r4[:, 256:256 + NL]
        r1 = cx.r4[:, 512:512 + NL]
        rs = cx.r4[:, 768:768 + NL]
        em.act(mu, cx.ps[0][0:1, 0:NL], AF.Copy, scale=1.0 / D)
        em.act(r0, cx.ps[1][0:1, 0:NL], AF.Copy, scale=1.0 / D)
        em.tt(r1, mu, mu, ALU.mult)
        em.tt(r0, r0, r1, ALU.subtract)
        em.act(r1, r0, AF.Ln, bias=cx.bcol("cst", 0)[0:1, :])
        em.act(rs, r1, AF.Exp, scale=-0.5)
        em.vcopy(cx.mu_bf, mu)
        em.vcopy(cx.rs_bf, rs)
        em.mm(cx.ps[2][:, 0:NL], cx.ones_row[:], cx.mu_bf)
        em.mm(cx.ps[3][:, 0:NL], cx.ones_row[:], cx.rs_bf)
        for t in range(ND):
            em.tt(xf(t), xf(t), cx.ps[2][:, 0:NL], ALU.subtract)
            em.tt(cx.xn[t][:, cs], xf(t), cx.ps[3][:, 0:NL], ALU.mult)


def bcast_col(cx, cols, bank, brow, bcsb=None, out_bf=True):
    em = cx.em
    pbf = cx.psbf(bank)
    for i, col in enumerate(cols):
        em.trans(pbf[0:1, i * 128:(i + 1) * 128], col, cx.id_t[:])
    em.act(brow[:], pbf[0:1, 0:512], AF.Copy)
    em.mm(cx.ps[bank][:], cx.ones_row[:], brow[:])
    if bcsb is not None:
        em.act(bcsb[:], cx.ps[bank][:], AF.Copy)
    return cx.ps[bank][:]


def accum_dma(cx, acc, c, half):
    em = cx.em
    src = cx.ybuf[:]
    dst = acc[half * 512:(half + 1) * 512, c * NT:(c + 1) * NT] \
        .rearrange("(a p) n -> p a n", p=128)
    s = src.rearrange("p (a n) -> p a n", n=NT)

    def f(e, dst=dst, s=s):
        return e.dma_start(dst, s, accum_op=ALU.add)
    return em.add("g", f, reads=[src], writes=[acc[:, :]], dma=True)



def l0_gates(cx):
    em = cx.em
    for k in range(ND):
        em.dma("g", cx.gsm[k][:, 0:16], cx.W["gw16"][k * 128:(k + 1) * 128, :])
    for c in range(NCH):
        for i in range(4):
            tok = slice(c * NT + i * 128, c * NT + (i + 1) * 128)
            for k in range(ND):
                em.mm(cx.ps[4][0:128, 0:16], cx.xn[k][:, tok], cx.gsm[k][:, 0:16],
                      start=(k == 0), stop=(k == ND - 1))
            g = cx.gtile
            em.act(g[:], cx.ps[4][0:128, 0:16], AF.Copy)
            col = lambda j: cx.gcol[:, j:j + 1]
            em.tt(g[:, 0:8], g[:, 0:8], cx.btile("bga", 8), ALU.add)
            em.reduce(col(0), g[:, 0:8], ALU.max, negate=True)
            ea = cx.gscr[:, 0:8]
            em.act(ea, g[:, 0:8], AF.Exp, bias=col(0))
            em.reduce(col(1), ea, ALU.add)
            em.recip(col(2), col(1))
            em.reduce(col(3), ea, ALU.max)
            em.tt(col(4), col(3), col(2), ALU.mult)
            msk = cx.gscr[:, 8:16]
            em.ts(msk, ea, col(3), None, op0=ALU.is_equal)
            em.ts(cx.selcw[:, (c * 4 + i) * 16:(c * 4 + i) * 16 + 8],
                  msk, col(4), None, op0=ALU.mult)
            em.reduce(col(0), g[:, 8:16], ALU.max, negate=True)
            ef = cx.gscr[:, 16:24]
            em.act(ef, g[:, 8:16], AF.Exp, bias=col(0))
            em.reduce(col(1), ef, ALU.add)
            em.recip(col(2), col(1))
            sc = cx.gscr[:, 24:32]
            em.ts(sc, ef, col(2), None, op0=ALU.mult)
            bi = cx.gscr[:, 32:40]
            em.tt(bi, sc, cx.btile("gb8", 8), ALU.add)
            em.reduce(col(5), bi, ALU.max)
            mk1 = cx.gscr[:, 0:8]
            em.ts(mk1, bi, col(5), None, op0=ALU.is_equal)
            bm = cx.gscr[:, 40:48]
            em.stt(bm, mk1, -1e30, bi, ALU.mult, ALU.add)
            em.reduce(col(6), bm, ALU.max)
            mk2 = cx.gscr[:, 8:16]
            em.ts(mk2, bm, col(6), None, op0=ALU.is_equal)
            em.tt(mk1, mk1, mk2, ALU.add)
            em.tt(cx.selcw[:, (c * 4 + i) * 16 + 8:(c * 4 + i) * 16 + 16],
                  mk1, sc, ALU.mult)


def l0_attn_experts(cx, acc):
    em = cx.em
    W = cx.W
    for e in range(8):
        for k in range(ND):
            em.dma("g", cx.WB[k][:], W["w1a"][e, k * 128:(k + 1) * 128, :])
        for f2 in range(8):
            em.dma("g", cx.WB[8 + f2][:, 0:1024],
                   W["w2a"][e, (2 * f2) * 128:(2 * f2 + 1) * 128, :])
            em.dma("g", cx.WB[8 + f2][:, 1024:2048],
                   W["w2a"][e, (2 * f2 + 1) * 128:(2 * f2 + 2) * 128, :])
        for c in range(NCH):
            cs = slice(c * NT, (c + 1) * NT)
            cols = [cx.selcw[:, (c * 4 + i) * 16 + e:(c * 4 + i) * 16 + e + 1]
                    for i in range(4)]
            bcast_col(cx, cols, 6, cx.brow[0], cx.bcsb[0])
            for fi in range(16):
                zb = cx.ps[4 + (fi % 2)]
                for k in range(ND):
                    em.mm(zb[:], cx.WB[k][:, fi * 128:(fi + 1) * 128],
                          cx.xn[k][:, cs], start=(k == 0), stop=(k == ND - 1))
                em.act(cx.hb[fi][:], zb[:], AF.Relu, bias=cx.bcol("b1a", e * 16 + fi))
            for di in range(8):
                yb = cx.ps[di % 4]
                for f in range(16):
                    f2, half = f // 2, f % 2
                    em.mm(yb[:], cx.WB[8 + f2][:, half * 1024 + di * 128:
                                               half * 1024 + (di + 1) * 128],
                          cx.hb[f][:], start=(f == 0), stop=(f == 15))
                em.stt(cx.ybuf[:, (di % 4) * NT:(di % 4 + 1) * NT], yb[:],
                       cx.bcol("b2a", e * 8 + di), cx.bcsb[0][:],
                       ALU.add, ALU.mult)
                if di % 4 == 3:
                    accum_dma(cx, acc, c, di // 4)


def _swiglu_phase1(cx, c, e_or_none, w1set, w3set, b1name, b3name, boff):
    em = cx.em
    cs = slice(c * NT, (c + 1) * NT)
    for fa in range(8):
        pg1, pg3, pl1, pl3 = cx.ps[4], cx.ps[5], cx.ps[6], cx.ps[7]
        for k in range(ND):
            st, sp = (k == 0), (k == ND - 1)
            em.mm(pg1[:], w1set[k][:, fa * 128:(fa + 1) * 128], cx.xn[k][:, cs], start=st, stop=sp)
        for k in range(ND):
            st, sp = (k == 0), (k == ND - 1)
            em.mm(pg3[:], w3set[k][:, fa * 128:(fa + 1) * 128], cx.xn[k][:, cs], start=st, stop=sp)
        for k in range(ND):
            st, sp = (k == 0), (k == ND - 1)
            em.mm(pl1[:], w1set[k][:, 1024 + fa * 128:1024 + (fa + 1) * 128], cx.xn[k][:, cs], start=st, stop=sp)
        for k in range(ND):
            st, sp = (k == 0), (k == ND - 1)
            em.mm(pl3[:], w3set[k][:, 1024 + fa * 128:1024 + (fa + 1) * 128], cx.xn[k][:, cs], start=st, stop=sp)
        em.act(cx.hb[8][:], pg1[:], AF.Identity, bias=cx.bcol(b1name, boff + fa))
        em.act(cx.hb[9][:], pg3[:], AF.Identity, bias=cx.bcol(b3name, boff + fa))
        em.act(cx.hb[10][:], pl1[:], AF.Identity, bias=cx.bcol(b1name, boff + 8 + fa))
        em.act(cx.hb[11][:], pl3[:], AF.Identity, bias=cx.bcol(b3name, boff + 8 + fa))
        em.tt(cx.fs[0][:], cx.hb[8][:], cx.hb[9][:], ALU.mult)
        em.tt(cx.fs[1][:], cx.hb[10][:], cx.hb[11][:], ALU.mult)
        em.ts(cx.fs[0][:], cx.fs[0][:], 7.0, None, op0=ALU.min)
        em.ts(cx.fs[1][:], cx.fs[1][:], 7.0, -7.0, op0=ALU.min, op1=ALU.max)
        em.act(cx.fs[2][:], cx.fs[0][:], AF.Sigmoid, scale=1.702)
        em.tt(cx.fs[0][:], cx.fs[0][:], cx.fs[2][:], ALU.mult)
        em.ts(cx.fs[1][:], cx.fs[1][:], 1.0, None, op0=ALU.add)
        em.tt(cx.hb[fa][:], cx.fs[0][:], cx.fs[1][:], ALU.mult)


def l0_ffn_experts(cx, acc):
    em = cx.em
    W = cx.W
    for e in range(8):
        for k in range(ND):
            em.dma("g", cx.WB[k][:], W["w1f"][e, k * 128:(k + 1) * 128, :])
            em.dma("g", cx.WB[8 + k][:], W["w3f"][e, k * 128:(k + 1) * 128, :])
            em.dma("g", cx.WC[k // 2][:, (k % 2) * 1024:(k % 2) * 1024 + 1024],
                   W["w2f"][e, k * 128:(k + 1) * 128, :])
        for c in range(NCH):
            cols = [cx.selcw[:, (c * 4 + i) * 16 + 8 + e:(c * 4 + i) * 16 + 9 + e]
                    for i in range(4)]
            bcast_col(cx, cols, 6, cx.brow[0], cx.bcsb[0])
            _swiglu_phase1(cx, c, e, cx.WB[0:8], cx.WB[8:16], "b1f", "b3f", e * 16)
            for di in range(8):
                yb = cx.ps[di % 4]
                for k in range(8):
                    em.mm(yb[:], cx.WC[k // 2][:, (k % 2) * 1024 + di * 128:
                                               (k % 2) * 1024 + (di + 1) * 128],
                          cx.hb[k][:], start=(k == 0), stop=(k == 7))
                em.stt(cx.ybuf[:, (di % 4) * NT:(di % 4 + 1) * NT], yb[:],
                       cx.bcol("b2f", e * 8 + di), cx.bcsb[0][:],
                       ALU.add, ALU.mult)
                if di % 4 == 3:
                    accum_dma(cx, acc, c, di // 4)


def l0_shared(cx, acc):
    em = cx.em
    W = cx.W
    for k in range(ND):
        em.dma("g", cx.WB[k][:], W["sw1f"][k * 128:(k + 1) * 128, :])
        em.dma("g", cx.WB[8 + k][:], W["sw3f"][k * 128:(k + 1) * 128, :])
        em.dma("g", cx.WC[k // 2][:, (k % 2) * 1024:(k % 2) * 1024 + 1024],
               W["sw2f"][k * 128:(k + 1) * 128, :])
    for c in range(NCH):
        _swiglu_phase1(cx, c, None, cx.WB[0:8], cx.WB[8:16], "sb1f", "sb3f", 0)
        for di in range(8):
            yb = cx.ps[di % 4]
            for k in range(8):
                em.mm(yb[:], cx.WC[k // 2][:, (k % 2) * 1024 + di * 128:
                                           (k % 2) * 1024 + (di + 1) * 128],
                      cx.hb[k][:], start=(k == 0), stop=(k == 7))
            em.act(cx.ybuf[:, (di % 4) * NT:(di % 4 + 1) * NT], yb[:], AF.Identity,
                   bias=cx.bcol("sb2", di))
            if di % 4 == 3:
                accum_dma(cx, acc, c, di // 4)


def emit_block0(cx, xsrc, acc):
    ln_pass(cx, xsrc, acc)
    l0_gates(cx)
    l0_attn_experts(cx, acc)
    l0_ffn_experts(cx, acc)
    l0_shared(cx, acc)



BLOCK_EMITTERS = {}


def build_program(nblocks=4):
    nc = bass.Bass("TRN2", target_bir_lowering=False)
    Wnp = _WSHAPES
    W = {}
    for name, (shape, dt) in Wnp.items():
        W[name] = nc.dram_tensor(name, list(shape), dt, kind="ExternalInput").ap()
    x_in = nc.dram_tensor("x", [D, T], F32, kind="ExternalInput").ap()
    W["hscr"] = nc.dram_tensor("hscr", [4096, T], BF16, kind="Internal").ap()
    W["thrscr"] = nc.dram_tensor("thrscr", [1, 1], F32, kind="Internal").ap()
    W["qkvscr"] = nc.dram_tensor("qkvscr", [3072, T], BF16, kind="Internal").ap()
    W["ctxscr"] = nc.dram_tensor("ctxscr", [1024, T], BF16, kind="Internal").ap()
    W["aoscr"] = nc.dram_tensor("aoscr", [1024, T], F32, kind="Internal").ap()
    W["pre2scr"] = nc.dram_tensor("pre2scr", [1024, T], F32, kind="Internal").ap()
    yout = nc.dram_tensor("yout", [D, T], F32, kind="ExternalOutput").ap()
    xs1 = nc.dram_tensor("xs1", [D, T], F32, kind="Internal").ap()
    xs2 = nc.dram_tensor("xs2", [D, T], F32, kind="Internal").ap()

    em = Em(nc)
    cx = Ctx(nc, em, W)
    srcs = [x_in, xs1, xs2, xs1]
    accs = [xs1, xs2, xs1, yout]
    accs[nblocks - 1] = yout
    emitters = [emit_block0, emit_block1, emit_block2, emit_block3][:nblocks]
    for b, fn in enumerate(emitters):
        fn(cx, srcs[b], accs[b])
    em.emit()
    return nc, em


_WSHAPES = None
_WPREPPED = None


def _prep(inputs):
    global _WSHAPES, _WPREPPED
    d = _prep_shared(inputs)
    _WPREPPED = d
    _WSHAPES = {}
    for k, v in d.items():
        dt = BF16 if v.dtype == ml_dtypes.bfloat16 else F32
        _WSHAPES[k] = (v.shape, dt)
    return d


def kernel(**inputs):
    d = _prep(inputs)
    x = np.asarray(inputs["L0_attn_x"], dtype=np.float32)
    nblocks = 4 if DEV_STOP_AFTER is None else DEV_STOP_AFTER + 1
    nc, em = build_program(nblocks=nblocks)
    in_maps = []
    for b in range(8):
        m = dict(d)
        m["x"] = np.ascontiguousarray(x[b].T)
        in_maps.append(m)
    res = run_bass_kernel_spmd(nc, in_maps, core_ids=list(range(8)))
    out = np.stack([np.asarray(r["yout"], dtype=np.float32).T
                    for r in res.results], axis=0)
    return out



def l1_attn_unit(cx, acc, t):
    em = cx.em

    def yb(i):
        return cx.ybuf[:, i * NT:(i + 1) * NT]
    wf = cx.bcol("wf", t)
    wi = cx.bcol("wi", t)
    wh = cx.bcol("wh", t)
    for c in range(NCH):
        cs = slice(c * NT, (c + 1) * NT)
        xnc = cx.xn[t][:, cs]
        f = cx.fsc[0][:]
        i_ = cx.fsc[1][:]
        em.ts(f, xnc, wf, None, op0=ALU.mult)
        em.ts(i_, xnc, wi, None, op0=ALU.mult)
        em.act(yb(0), f, AF.Exp, scale=-1.0)
        em.act(yb(0), yb(0), AF.Ln, bias=cx.bcol("cst", 3))
        em.act(yb(1), i_, AF.Exp, scale=-1.0)
        em.act(yb(1), yb(1), AF.Ln, bias=cx.bcol("cst", 3))
        em.tt(f, yb(0), yb(1), ALU.subtract)
        em.act(yb(0), f, AF.Exp)
        em.ts(yb(0), yb(0), 1.0, None, op0=ALU.add)
        em.recip(yb(3), yb(0))
        em.act(yb(0), f, AF.Exp, scale=-1.0)
        em.ts(yb(0), yb(0), 1.0, None, op0=ALU.add)
        em.recip(i_, yb(0))
        em.ts(yb(0), xnc, wh, None, op0=ALU.mult)
        em.act(yb(1), yb(0), AF.Exp, scale=-1.0)
        em.ts(yb(1), yb(1), 1.0, None, op0=ALU.add)
        em.recip(yb(2), yb(1))
        em.ts(yb(1), yb(0), 0.5, None, op0=ALU.add)
        em.ts(yb(0), yb(0), 0.0, None, op0=ALU.is_ge)
        em.tt(yb(1), yb(1), yb(2), ALU.subtract)
        em.tt(yb(1), yb(0), yb(1), ALU.mult)
        em.tt(f, yb(1), yb(2), ALU.add)
        em.tt(i_, i_, f, ALU.mult)
        init = 0.0 if c == 0 else cx.gcol[:, 0:1]
        em.scan(yb(0), yb(3), i_, init, op0=ALU.mult, op1=ALU.add)
        em.vcopy(cx.gcol[:, 0:1], yb(0)[:, NT - 1:NT])
        dst = acc[t * 128:(t + 1) * 128, cs]

        def fdma(e, dst=dst, s=yb(0)):
            return e.dma_start(dst, s, accum_op=ALU.add)
        em.add("g", fdma, reads=[yb(0)], writes=[acc[:, :]], dma=True)


def _absmean_quant(cx, wdram, nk, dst_tiles):
    em = cx.em
    F = wdram.shape[1]
    SW = min(F, 2048)
    nh = F // SW
    colsum = cx.gcol[:, 1:2]
    first = True
    for k in range(nk):
        for h2 in range(nh):
            em.dma("y", cx.ybuf[:, 0:SW], wdram[k * 128:(k + 1) * 128,
                                                h2 * SW:(h2 + 1) * SW])
            em.reduce(cx.gcol[:, 2:3], cx.ybuf[:, 0:SW], ALU.add, absval=True)
            if first:
                em.vcopy(colsum, cx.gcol[:, 2:3])
                first = False
            else:
                em.tt(colsum, colsum, cx.gcol[:, 2:3], ALU.add)
    em.trans(cx.ps[7][0:1, 0:128], colsum, cx.id_f[:])
    em.act(cx.r4[:, 0:128], cx.ps[7][0:1, 0:128], AF.Copy)
    em.reduce(cx.r4[:, 128:129], cx.r4[:, 0:128], ALU.add)
    n = float(128 * nk * F)
    em.ts(cx.r4[:, 129:130], cx.r4[:, 128:129], 0.5 / n, 0.5e-6,
          op0=ALU.mult, op1=ALU.add)
    em.mm(cx.ps[7][:, 0:1], cx.ones_rf[:], cx.r4[:, 129:130])
    em.act(cx.gcol[:, 3:4], cx.ps[7][:, 0:1], AF.Copy)
    thr = cx.gcol[:, 3:4]
    for k in range(nk):
        for h2 in range(nh):
            em.dma("y", cx.ybuf[:, 0:SW], wdram[k * 128:(k + 1) * 128,
                                                h2 * SW:(h2 + 1) * SW])
            sgn = cx.fs[0][:]
            for q in range(SW // NT):
                qs = slice(q * NT, (q + 1) * NT)
                em.ts(sgn, cx.ybuf[:, qs], 0.0, None, op0=ALU.is_gt)
                em.ts(sgn, sgn, 2.0, -1.0, op0=ALU.mult, op1=ALU.add)
                em.act(cx.fs[1][:], cx.ybuf[:, qs], AF.Abs)
                em.ts(cx.fs[1][:], cx.fs[1][:], thr, None, op0=ALU.is_gt)
                dt, dcols = dst_tiles[k * nh + h2]
                assert dcols.stop - dcols.start == SW
                em.tt(dt[:, dcols.start + q * NT:dcols.start + (q + 1) * NT],
                      sgn, cx.fs[1][:], ALU.mult)


def l1_quant_wfc(cx):
    dst = []
    for k in range(ND):
        dst.append((cx.WB[2 * k], slice(0, 2048)))
        dst.append((cx.WB[2 * k + 1], slice(0, 2048)))
    _absmean_quant(cx, cx.W["wfc"], ND, dst)


def _l1_wfc_tile(cx, k, hi):
    tens = cx.WB[2 * k + (hi // 16)]
    return tens[:, (hi % 16) * 128:((hi % 16) + 1) * 128]


def l1_h_chunk(cx, c):
    em = cx.em
    W = cx.W
    NL = 256
    cs = slice(c * NL, (c + 1) * NL)
    for hi in range(32):
        zb = cx.ps[4 + hi % 2]
        for k in range(ND):
            em.mm(zb[0:128, 0:NL], _l1_wfc_tile(cx, k, hi), cx.xn[k][:, cs],
                  start=(k == 0), stop=(k == ND - 1))
        em.act(cx.hb[hi // 2][:, (hi % 2) * NL:(hi % 2 + 1) * NL],
               zb[0:128, 0:NL], AF.Gelu)
    for hi in range(32):
        hsl = cx.hb[hi // 2][:, (hi % 2) * NL:(hi % 2 + 1) * NL]
        em.mm(cx.ps[0][0:1, 0:NL], cx.ones_col[:], hsl,
              start=(hi == 0), stop=(hi == 31))
    for hi in range(32):
        hsl = cx.hb[hi // 2][:, (hi % 2) * NL:(hi % 2 + 1) * NL]
        sq = cx.bcsb[hi % 2]
        em.act(sq[:, 0:NL], hsl, AF.Square)
        em.mm(cx.ps[1][0:1, 0:NL], cx.ones_col[:], sq[:, 0:NL],
              start=(hi == 0), stop=(hi == 31))
    mu = cx.r4[:, 0:NL]
    r0 = cx.r4[:, 256:256 + NL]
    r1 = cx.r4[:, 512:512 + NL]
    rs = cx.r4[:, 768:768 + NL]
    em.act(mu, cx.ps[0][0:1, 0:NL], AF.Copy, scale=1.0 / 4096)
    em.act(r0, cx.ps[1][0:1, 0:NL], AF.Copy, scale=1.0 / 4096)
    em.tt(r1, mu, mu, ALU.mult)
    em.tt(r0, r0, r1, ALU.subtract)
    em.act(r1, r0, AF.Ln, bias=cx.bcol("cst", 0)[0:1, :])
    em.act(rs, r1, AF.Exp, scale=-0.5)
    em.vcopy(cx.mu_bf, mu)
    em.vcopy(cx.rs_bf, rs)
    em.mm(cx.ps[2][0:128, 0:NL], cx.ones_row[:], cx.mu_bf)
    em.mm(cx.ps[3][0:128, 0:NL], cx.ones_row[:], cx.rs_bf)
    for hi in range(32):
        hsl = cx.hb[hi // 2][:, (hi % 2) * NL:(hi % 2 + 1) * NL]
        em.tt(cx.fs[0][:, 0:NL], hsl, cx.ps[2][0:128, 0:NL], ALU.subtract)
        em.tt(cx.fs[0][:, 0:NL], cx.fs[0][:, 0:NL], cx.ps[3][0:128, 0:NL],
              ALU.mult)
        em.ts(cx.fs[1][:, 0:NL].bitcast(BF16)[:, 0:NL], cx.fs[0][:, 0:NL],
              cx.bcol("lnw1", hi), cx.bcol("lnb1", hi),
              op0=ALU.mult, op1=ALU.add)
        em.dma("s", W["hscr"][hi * 128:(hi + 1) * 128, cs],
               cx.fs[1][:, 0:NL].bitcast(BF16)[:, 0:NL])


def l1_proj(cx, acc):
    em = cx.em
    W = cx.W
    NL = 256
    dst = []
    for k in range(32):
        tens = cx.WB[k // 2]
        dst.append((tens, slice((k % 2) * 1024, (k % 2) * 1024 + 1024)))
    _absmean_quant(cx, W["wproj"], 32, dst)

    def wproj_tile(k, di):
        tens = cx.WB[k // 2]
        return tens[:, (k % 2) * 1024 + di * 128:(k % 2) * 1024 + (di + 1) * 128]

    def slot(c, k):
        if c % 2 == 0:
            return cx.hb[k // 2][:, (k % 2) * NL:(k % 2 + 1) * NL]
        base = ((k // 2) % 2) * 1024 + (k % 2) * NL
        return cx.WC[k // 4][:, base:base + NL]

    for c in range(T // NL):
        cs = slice(c * NL, (c + 1) * NL)
        for kk in range(16):
            src = W["hscr"][kk * 256:(kk + 1) * 256, cs] \
                .rearrange("(a p) n -> p a n", p=128)
            d0 = slot(c, 2 * kk)
            d1 = slot(c, 2 * kk + 1)
            if c % 2 == 0:
                em.dma("g", cx.hb[kk][:].rearrange("p (a n) -> p a n", n=NL), src)
            else:
                em.dma("g", cx.WC[kk // 2][:, (kk % 2) * 1024:(kk % 2) * 1024 + 512]
                       .rearrange("p (a n) -> p a n", n=NL), src)
        for di in range(ND):
            zb = cx.ps[di % 4]
            for k in range(32):
                em.mm(zb[0:128, 0:NL], wproj_tile(k, di), slot(c, k),
                      start=(k == 0), stop=(k == 31))
            em.act(cx.ybuf[:, (di % 4) * NL:(di % 4 + 1) * NL], zb[0:128, 0:NL],
                   AF.Copy)
            if di % 4 == 3:
                half = di // 4
                srcb = cx.ybuf[:, 0:4 * NL]
                dst2 = acc[half * 512:(half + 1) * 512, cs] \
                    .rearrange("(a p) n -> p a n", p=128)
                s2 = srcb.rearrange("p (a n) -> p a n", n=NL)

                def fdma(e, dst2=dst2, s2=s2):
                    return e.dma_start(dst2, s2, accum_op=ALU.add)
                em.add("g", fdma, reads=[srcb], writes=[acc[:, :]], dma=True)


def emit_block1(cx, xsrc, acc):
    ln_pass(cx, xsrc, acc)
    l1_quant_wfc(cx)
    for i in range(16):
        l1_h_chunk(cx, i)
        if i % 2 == 1:
            l1_attn_unit(cx, acc, i // 2)
    l1_proj(cx, acc)



def l2_attn(cx, acc):
    em = cx.em
    W = cx.W
    qkv = W["qkvscr"]
    ctxs = W["ctxscr"]
    for k in range(ND):
        em.dma("g", cx.WB[k // 2][:, (k % 2) * 1024:(k % 2) * 1024 + 1024],
               W["wq2"][k * 128:(k + 1) * 128, :])
        em.dma("g", cx.WB[4][:, k * 256:(k + 1) * 256],
               W["wka"][k * 128:(k + 1) * 128, :])
        em.dma("g", cx.WB[5][:, k * 256:(k + 1) * 256],
               W["wva"][k * 128:(k + 1) * 128, :])
        em.dma("g", cx.WB[8 + k // 2][:, (k % 2) * 1024:(k % 2) * 1024 + 1024],
               W["wo2"][k * 128:(k + 1) * 128, :])
    for k in range(2):
        em.dma("g", cx.WB[6][:, k * 1024:(k + 1) * 1024],
               W["wkb"][k * 128:(k + 1) * 128, :])
        em.dma("g", cx.WB[7][:, k * 1024:(k + 1) * 1024],
               W["wvb"][k * 128:(k + 1) * 128, :])
    em.dma("g", cx.WB[12][:, 0:1024], W["ropeT2"])

    def _rope_store(c, base_row):
        cs = slice(c * NT, (c + 1) * NT)
        for ot in range(8):
            zb = cx.ps[6]
            for hh in range(2):
                em.mm(zb[hh * 64:(hh + 1) * 64, :],
                      cx.WB[12][hh * 64:(hh + 1) * 64, ot * 64:(ot + 1) * 64],
                      cx.hb[ot][hh * 64:(hh + 1) * 64, :])
            em.act(cx.hb[8][:], zb[:], AF.Copy)
            em.dma("s", W["qkvscr"][base_row + ot * 128:base_row + (ot + 1) * 128, cs],
                   cx.hb[8][:])

    for c in range(NCH):
        cs = slice(c * NT, (c + 1) * NT)
        for ot in range(8):
            zb = cx.ps[4 + ot % 2]
            for k in range(ND):
                em.mm(zb[:], cx.WB[k // 2][:, (k % 2) * 1024 + ot * 128:
                                           (k % 2) * 1024 + (ot + 1) * 128],
                      cx.xn[k][:, cs], start=(k == 0), stop=(k == ND - 1))
            em.act(cx.hb[ot][:], zb[:], AF.Copy)
        _rope_store(c, 0)
        for lt in range(2):
            zb = cx.ps[4 + lt]
            for k in range(ND):
                em.mm(zb[:], cx.WB[4][:, k * 256 + lt * 128:k * 256 + (lt + 1) * 128],
                      cx.xn[k][:, cs], start=(k == 0), stop=(k == ND - 1))
            em.act(cx.hb[10 + lt][:], zb[:], AF.Copy)
        for ot in range(8):
            zb = cx.ps[4 + ot % 2]
            for lt in range(2):
                em.mm(zb[:], cx.WB[6][:, lt * 1024 + ot * 128:
                                      lt * 1024 + (ot + 1) * 128],
                      cx.hb[10 + lt][:], start=(lt == 0), stop=(lt == 1))
            em.act(cx.hb[ot][:], zb[:], AF.Copy)
        _rope_store(c, 1024)
        for lt in range(2):
            zb = cx.ps[4 + lt]
            for k in range(ND):
                em.mm(zb[:], cx.WB[5][:, k * 256 + lt * 128:k * 256 + (lt + 1) * 128],
                      cx.xn[k][:, cs], start=(k == 0), stop=(k == ND - 1))
            em.act(cx.hb[10 + lt][:], zb[:], AF.Copy)
        for ot in range(8):
            zb = cx.ps[4 + ot % 2]
            for lt in range(2):
                em.mm(zb[:], cx.WB[7][:, lt * 1024 + ot * 128:
                                      lt * 1024 + (ot + 1) * 128],
                      cx.hb[10 + lt][:], start=(lt == 0), stop=(lt == 1))
            em.act(cx.hb[8][:], zb[:], AF.Copy)
            em.dma("s", qkv[2048 + ot * 128:2048 + (ot + 1) * 128, cs], cx.hb[8][:])

    for h in range(16):
        for half in range(2):
            hs = slice(half * 2048, (half + 1) * 2048)
            em.dma("g", cx.WB[13][half * 64:(half + 1) * 64, :],
                   qkv[1024 + h * 64:1024 + (h + 1) * 64, hs])
            em.dma("g", cx.WB[14][half * 64:(half + 1) * 64, :],
                   qkv[2048 + h * 64:2048 + (h + 1) * 64, hs])
            em.dma("g", cx.WB[15][half * 64:(half + 1) * 64, :],
                   qkv[h * 64:(h + 1) * 64, hs])
            em.dma("g", cx.WC[0][(1 - half) * 64:(2 - half) * 64, :],
                   qkv[h * 64:(h + 1) * 64, hs])
        def vtile(j, w=65):
            if j < 31:
                return cx.WB[12][:, j * 66:j * 66 + w]
            return cx.bcsb[1][:, 0:w]
        em.memset(cx.WB[12][:], 1.0, eng="v")
        em.memset(cx.bcsb[1][:, 0:66], 1.0, eng="v")
        for j in range(32):
            b_j = (j // 16) * 64
            vsl = cx.WB[14][b_j:b_j + 64, (j % 16) * 128:(j % 16 + 1) * 128]
            em.trans(cx.psbf(2)[0:128, 0:64], vsl,
                     cx.id_t[b_j:b_j + 64, b_j:b_j + 64])
            em.act(vtile(j, 64), cx.psbf(2)[0:128, 0:64], AF.Copy)
        for qc in range(NCH):
            for j2 in range(16):
                for jj in range(2):
                    j = 2 * j2 + jj
                    b_j = (j // 16) * 64
                    qsrc = cx.WB[15] if (qc // 4) * 64 == b_j else cx.WC[0]
                    qrhs = qsrc[b_j:b_j + 64, (qc % 4) * 512:(qc % 4 + 1) * 512]
                    ktile = cx.WB[13][b_j:b_j + 64,
                                      (j % 16) * 128:(j % 16 + 1) * 128]
                    em.mm(cx.ps[4 + 2 * (j2 % 2) + jj], ktile, qrhs)
                pt = cx.WC[1][:, (j2 % 2) * 1024:(j2 % 2 + 1) * 1024]
                em.act(pt, cx.psS[j2 % 2], AF.Exp, scale=0.125)
                for jj in range(2):
                    j = 2 * j2 + jj
                    em.mm(cx.ps[0][0:65, :], vtile(j),
                          pt[:, jj * 512:(jj + 1) * 512],
                          start=(j == 0), stop=(j == 31))
            zr = cx.r4[:, 0:NT]
            em.recip(zr, cx.ps[0][64:65, :])
            em.vcopy(cx.brow[0], zr)
            em.mm(cx.ps[2][:], cx.ones_row[:], cx.brow[0])
            em.act(cx.fs[0][0:64, :], cx.ps[0][0:64, :], AF.Copy)
            em.tt(cx.bcsb[0][0:64, :], cx.fs[0][0:64, :], cx.ps[2][0:64, :],
                  ALU.mult)
            em.dma("s", ctxs[h * 64:(h + 1) * 64, qc * NT:(qc + 1) * NT],
                   cx.bcsb[0][0:64, :])

    for c in range(NCH):
        cs = slice(c * NT, (c + 1) * NT)
        for k in range(ND):
            em.dma("g", cx.hb[k][:], ctxs[k * 128:(k + 1) * 128, cs])
        for di in range(8):
            yb = cx.ps[di % 4]
            for k in range(ND):
                em.mm(yb[:], cx.WB[8 + k // 2][:, (k % 2) * 1024 + di * 128:
                                               (k % 2) * 1024 + (di + 1) * 128],
                      cx.hb[k][:], start=(k == 0), stop=(k == ND - 1))
            em.act(cx.ybuf[:, (di % 4) * NT:(di % 4 + 1) * NT], yb[:], AF.Copy)
            if di % 4 == 3:
                accum_dma(cx, acc, c, di // 4)


def l2_gates(cx):
    em = cx.em
    for k in range(ND):
        em.dma("g", cx.gsm[k][:, 0:8], cx.W["rw8"][k * 128:(k + 1) * 128, :])
    for c in range(NCH):
        for i in range(4):
            tok = slice(c * NT + i * 128, c * NT + (i + 1) * 128)
            for k in range(ND):
                em.mm(cx.ps[4][0:128, 0:8], cx.xn[k][:, tok], cx.gsm[k][:, 0:8],
                      start=(k == 0), stop=(k == ND - 1))
            g = cx.gtile
            em.act(g[:, 0:8], cx.ps[4][0:128, 0:8], AF.Copy)
            col = lambda j: cx.gcol[:, j:j + 1]
            em.tt(g[:, 0:8], g[:, 0:8], cx.btile("rb8", 8), ALU.add)
            em.reduce(col(0), g[:, 0:8], ALU.max, negate=True)
            ef = cx.gscr[:, 16:24]
            em.act(ef, g[:, 0:8], AF.Exp, bias=col(0))
            em.reduce(col(1), ef, ALU.add)
            em.recip(col(2), col(1))
            sc = cx.gscr[:, 24:32]
            em.ts(sc, ef, col(2), None, op0=ALU.mult)
            em.reduce(col(5), sc, ALU.max)
            mk1 = cx.gscr[:, 0:8]
            em.ts(mk1, sc, col(5), None, op0=ALU.is_equal)
            bm = cx.gscr[:, 40:48]
            em.stt(bm, mk1, -1e30, sc, ALU.mult, ALU.add)
            em.reduce(col(6), bm, ALU.max)
            mk2 = cx.gscr[:, 8:16]
            em.ts(mk2, bm, col(6), None, op0=ALU.is_equal)
            em.tt(mk1, mk1, mk2, ALU.add)
            em.tt(mk1, mk1, sc, ALU.mult)
            em.tt(col(5), col(5), col(6), ALU.add)
            em.recip(col(4), col(5))
            em.ts(cx.selcw[:, (c * 4 + i) * 16 + 8:(c * 4 + i) * 16 + 16],
                  mk1, col(4), None, op0=ALU.mult)


def l2_ffn_experts(cx, acc):
    em = cx.em
    W = cx.W
    for e in range(8):
        for k in range(ND):
            em.dma("g", cx.WB[k][:], W["ew1"][e, k * 128:(k + 1) * 128, :])
            em.dma("g", cx.WB[8 + k][:], W["ew2"][e, k * 128:(k + 1) * 128, :])
        for k in range(16):
            em.dma("g", cx.WC[k // 2][:, (k % 2) * 1024:(k % 2) * 1024 + 1024],
                   W["ew3"][e, k * 128:(k + 1) * 128, :])
        for c in range(NCH):
            cs = slice(c * NT, (c + 1) * NT)
            cols = [cx.selcw[:, (c * 4 + i) * 16 + 8 + e:(c * 4 + i) * 16 + 9 + e]
                    for i in range(4)]
            bcast_col(cx, cols, 6, cx.brow[0], cx.bcsb[0])
            for fi in range(16):
                pa, pb = cx.ps[4], cx.ps[5]
                for k in range(ND):
                    em.mm(pa[:], cx.WB[k][:, fi * 128:(fi + 1) * 128],
                          cx.xn[k][:, cs], start=(k == 0), stop=(k == ND - 1))
                for k in range(ND):
                    em.mm(pb[:], cx.WB[8 + k][:, fi * 128:(fi + 1) * 128],
                          cx.xn[k][:, cs], start=(k == 0), stop=(k == ND - 1))
                em.act(cx.bcsb[1][:], pa[:], AF.Silu)
                em.tt(cx.hb[fi][:], pb[:], cx.bcsb[1][:], ALU.mult)
            for di in range(8):
                yb = cx.ps[di % 4]
                for f in range(16):
                    em.mm(yb[:], cx.WC[f // 2][:, (f % 2) * 1024 + di * 128:
                                               (f % 2) * 1024 + (di + 1) * 128],
                          cx.hb[f][:], start=(f == 0), stop=(f == 15))
                em.tt(cx.ybuf[:, (di % 4) * NT:(di % 4 + 1) * NT], yb[:],
                      cx.bcsb[0][:], ALU.mult)
                if di % 4 == 3:
                    accum_dma(cx, acc, c, di // 4)


def l2_shared(cx, acc):
    em = cx.em
    W = cx.W
    for k in range(ND):
        em.dma("g", cx.WB[k][:], W["sw1_2"][k * 128:(k + 1) * 128, :])
        em.dma("g", cx.WB[8 + k][:], W["sw2_2"][k * 128:(k + 1) * 128, :])
    for k in range(16):
        em.dma("g", cx.WC[k // 2][:, (k % 2) * 1024:(k % 2) * 1024 + 1024],
               W["sw3_2"][k * 128:(k + 1) * 128, :])
    for c in range(NCH):
        cs = slice(c * NT, (c + 1) * NT)
        for fi in range(16):
            pa, pb = cx.ps[4], cx.ps[5]
            for k in range(ND):
                em.mm(pa[:], cx.WB[k][:, fi * 128:(fi + 1) * 128],
                      cx.xn[k][:, cs], start=(k == 0), stop=(k == ND - 1))
            for k in range(ND):
                em.mm(pb[:], cx.WB[8 + k][:, fi * 128:(fi + 1) * 128],
                      cx.xn[k][:, cs], start=(k == 0), stop=(k == ND - 1))
            em.act(cx.bcsb[1][:], pa[:], AF.Silu)
            em.tt(cx.hb[fi][:], pb[:], cx.bcsb[1][:], ALU.mult)
        for di in range(8):
            yb = cx.ps[di % 4]
            for f in range(16):
                em.mm(yb[:], cx.WC[f // 2][:, (f % 2) * 1024 + di * 128:
                                           (f % 2) * 1024 + (di + 1) * 128],
                      cx.hb[f][:], start=(f == 0), stop=(f == 15))
            em.act(cx.ybuf[:, (di % 4) * NT:(di % 4 + 1) * NT], yb[:], AF.Copy)
            if di % 4 == 3:
                accum_dma(cx, acc, c, di // 4)


def emit_block2(cx, xsrc, acc):
    ln_pass(cx, xsrc, acc)
    l2_gates(cx)
    l2_attn(cx, acc)
    l2_ffn_experts(cx, acc)
    l2_shared(cx, acc)




def l3_attn(cx, acc):
    em = cx.em
    W = cx.W
    qn_scr = W["qkvscr"]
    aoscr = W["aoscr"]
    pre2 = W["pre2scr"]
    for k in range(ND):
        em.dma("g", cx.WB[k // 2][:, (k % 2) * 1024:(k % 2) * 1024 + 1024],
               W["wq3"][k * 128:(k + 1) * 128, :])
        em.dma("g", cx.WB[4 + k // 2][:, (k % 2) * 1024:(k % 2) * 1024 + 1024],
               W["wk3"][k * 128:(k + 1) * 128, :])
        em.dma("g", cx.WB[8 + k // 2][:, (k % 2) * 1024:(k % 2) * 1024 + 1024],
               W["wv3"][k * 128:(k + 1) * 128, :])
        em.dma("g", cx.WC[k // 2][:, (k % 2) * 1024:(k % 2) * 1024 + 1024],
               W["wo3"][k * 128:(k + 1) * 128, :])
    kvsb = cx.WB[12]

    def proj(wbase, bname, ot, cs, zb):
        for k in range(ND):
            em.mm(zb[:], cx.WB[wbase + k // 2][:, (k % 2) * 1024 + ot * 128:
                                               (k % 2) * 1024 + (ot + 1) * 128],
                  cx.xn[k][:, cs], start=(k == 0), stop=(k == ND - 1))

    first_kv = [True, True]
    for c in range(NCH):
        cs = slice(c * NT, (c + 1) * NT)
        for ot in range(8):
            zb = cx.ps[4 + ot % 2]
            proj(0, "bq", ot, cs, zb)
            bq = cx.bcol("bq", ot)
            em.ts(cx.fs[0][:], zb[:], bq, 0.0, op0=ALU.add, op1=ALU.min)
            em.act(cx.fs[1][:], cx.fs[0][:], AF.Exp)
            em.ts(cx.fs[2][:], zb[:], bq, 0.0, op0=ALU.add, op1=ALU.max)
            em.stt(cx.hb[ot][:], cx.fs[2][:], -1.0, cx.fs[1][:],
                   ALU.add, ALU.add)
        for ot in range(8):
            em.act(cx.bcsb[1][:], cx.hb[ot][:], AF.Square)
            for hh in range(2):
                b = hh * 64
                em.mm(cx.ps[6][0:1, :],
                      cx.ones_col[0:64] if b == 0 else cx.ones_col[64:128],
                      cx.bcsb[1][b:b + 64, :])
                r0 = cx.r4[:, 0:NT]
                em.act(r0, cx.ps[6][0:1, :], AF.Ln)
                em.act(r0, r0, AF.Exp, scale=-0.5)
                em.vcopy(cx.brow[0], r0)
                em.mm(cx.ps[6][:], cx.ones_row[:], cx.brow[0])
                em.tt(cx.hb[ot][b:b + 64, :], cx.hb[ot][b:b + 64, :],
                      cx.ps[6][b:b + 64, :], ALU.mult)
            em.dma("s", qn_scr[ot * 128:(ot + 1) * 128, cs], cx.hb[ot][:])
        for ot in range(8):
            zb = cx.ps[4 + ot % 2]
            proj(4, "bk", ot, cs, zb)
            bk = cx.bcol("bk", ot)
            em.ts(cx.fs[0][:], zb[:], bk, 0.0, op0=ALU.add, op1=ALU.min)
            em.act(cx.fs[1][:], cx.fs[0][:], AF.Exp)
            em.ts(cx.fs[2][:], zb[:], bk, 0.0, op0=ALU.add, op1=ALU.max)
            em.stt(cx.hb[ot][:], cx.fs[2][:], -1.0, cx.fs[1][:],
                   ALU.add, ALU.add)
        for ot in range(8):
            zb = cx.ps[4 + ot % 2]
            proj(8, "bv", ot, cs, zb)
            em.act(cx.hb[8 + ot][:], zb[:], AF.Identity,
                   bias=cx.bcol("bv", ot))
        for h in range(16):
            b = (h % 2) * 64
            ident = cx.id_t[b:b + 64, b:b + 64]
            for i in range(4):
                tsl = slice(i * 128, (i + 1) * 128)
                em.trans(cx.psbf(2)[0:128, 0:64],
                         cx.hb[h // 2][b:b + 64, tsl], ident)
                em.act(cx.fs[0][:, 0:64], cx.psbf(2)[0:128, 0:64], AF.Copy)
                em.trans(cx.psbf(3)[0:128, 0:64],
                         cx.hb[8 + h // 2][b:b + 64, tsl], ident)
                em.act(cx.bcsb[1][:, 64:128], cx.psbf(3)[0:128, 0:64],
                       AF.Copy)
                em.act(cx.fs[1][:, 0:64], cx.fs[0][:, 0:64], AF.Square)
                em.reduce(cx.gcol[:, 4:5], cx.fs[1][:, 0:64], ALU.add)
                em.act(cx.gcol[:, 5:6], cx.gcol[:, 4:5], AF.Ln)
                em.act(cx.gcol[:, 5:6], cx.gcol[:, 5:6], AF.Exp, scale=-0.5)
                em.ts(cx.bcsb[1][:, 0:64], cx.fs[0][:, 0:64],
                      cx.gcol[:, 5:6], None, op0=ALU.mult)
                bank = cx.ps[h // 8]
                em.mm(bank[0:64, (h % 8) * 64:(h % 8) * 64 + 64],
                      cx.bcsb[1][:, 0:64], cx.bcsb[1][:, 64:128],
                      start=first_kv[h // 8], stop=False)
                first_kv[h // 8] = False
    for h in range(16):
        src = cx.ps[h // 8][0:64, (h % 8) * 64:(h % 8) * 64 + 64]
        em.act(kvsb[0:64, h * 64:(h + 1) * 64], src, AF.Copy)
        em.act(kvsb[64:128, h * 64:(h + 1) * 64], src, AF.Copy)

    NL = 256
    for c in range(T // NL):
        cs = slice(c * NL, (c + 1) * NL)
        for t in range(ND):
            em.dma("g", cx.hb[t][:, 0:NL], qn_scr[t * 128:(t + 1) * 128, cs])
        for h in range(16):
            b = (h % 2) * 64
            em.mm(cx.ps[4 + (h // 2) % 2][b:b + 64, 0:NL],
                  kvsb[b:b + 64, h * 64:(h + 1) * 64],
                  cx.hb[h // 2][b:b + 64, 0:NL])
            if h % 2 == 1:
                em.act(cx.hb[8 + h // 2][:, 0:NL],
                       cx.ps[4 + (h // 2) % 2][:, 0:NL], AF.Copy,
                       scale=0.125)
        for di in range(8):
            yb = cx.ps[di % 4]
            for k in range(ND):
                em.mm(yb[:, 0:NL], cx.WC[k // 2][:, (k % 2) * 1024 + di * 128:
                                                 (k % 2) * 1024 + (di + 1) * 128],
                      cx.hb[8 + k][:, 0:NL], start=(k == 0), stop=(k == ND - 1))
            em.stt(cx.ybuf[:, di * NL:(di + 1) * NL], yb[:, 0:NL],
                   cx.bcol("bo", di), cx.xn[di][:, cs], ALU.add, ALU.add)
        for t in range(ND):
            em.vcopy(cx.hb[t][:, 0:NL], cx.ybuf[:, t * NL:(t + 1) * NL])
            em.mm(cx.ps[4][0:1, 0:NL], cx.ones_col[:], cx.hb[t][:, 0:NL],
                  start=(t == 0), stop=(t == ND - 1))
        for t in range(ND):
            em.act(cx.bcsb[1][:, 0:NL], cx.hb[t][:, 0:NL], AF.Square)
            em.mm(cx.ps[5][0:1, 0:NL], cx.ones_col[:], cx.bcsb[1][:, 0:NL],
                  start=(t == 0), stop=(t == ND - 1))
        mu = cx.r4[:, 0:NL]
        r0 = cx.r4[:, 256:256 + NL]
        r1 = cx.r4[:, 512:512 + NL]
        rs = cx.r4[:, 768:768 + NL]
        em.act(mu, cx.ps[4][0:1, 0:NL], AF.Copy, scale=1.0 / D)
        em.act(r0, cx.ps[5][0:1, 0:NL], AF.Copy, scale=1.0 / D)
        em.tt(r1, mu, mu, ALU.mult)
        em.tt(r0, r0, r1, ALU.subtract)
        em.act(r1, r0, AF.Ln, bias=cx.bcol("cst", 1)[0:1, :])
        em.act(rs, r1, AF.Exp, scale=-0.5)
        em.vcopy(cx.mu_bf, mu)
        em.vcopy(cx.rs_bf, rs)
        em.mm(cx.ps[2][:, 0:NL], cx.ones_row[:], cx.mu_bf)
        em.mm(cx.ps[3][:, 0:NL], cx.ones_row[:], cx.rs_bf)
        for t in range(ND):
            sl = cx.ybuf[:, t * NL:(t + 1) * NL]
            em.tt(sl, sl, cx.ps[2][:, 0:NL], ALU.subtract)
            em.tt(sl, sl, cx.ps[3][:, 0:NL], ALU.mult)
            em.ts(cx.hb[t][:, 0:NL], sl, cx.bcol("g1", t), cx.bcol("be1", t),
                  op0=ALU.mult, op1=ALU.add)
            em.ts(sl, sl, cx.bcol("g1", t), cx.bcol("be1", t),
                  op0=ALU.mult, op1=ALU.add)
            em.dma("s", qn_scr[1024 + t * 128:1024 + (t + 1) * 128, cs],
                   cx.hb[t][:, 0:NL])
            em.dma("s", aoscr[t * 128:(t + 1) * 128, cs], sl)


def l3_b2(cx, W, qn_scr, pre2):
    em = cx.em
    for half in range(2):
        fo = half * 2048
        for k in range(ND):
            em.dma("g", cx.WB[k][:],
                   W["w1_3"][k * 128:(k + 1) * 128, fo:fo + 2048])
        for k in range(16):
            em.dma("g", cx.WC[k // 2][:, (k % 2) * 1024:(k % 2) * 1024 + 1024],
                   W["w2_3"][fo + k * 128:fo + (k + 1) * 128, :])
        for c in range(NCH):
            cs = slice(c * NT, (c + 1) * NT)
            for t in range(ND):
                em.dma("g", cx.hb[t][:],
                       qn_scr[1024 + t * 128:1024 + (t + 1) * 128, cs])
            for sub in range(2):
                for fi8 in range(8):
                    fi = sub * 8 + fi8
                    zb = cx.ps[4 + fi % 2]
                    for k in range(ND):
                        em.mm(zb[:], cx.WB[k][:, fi * 128:(fi + 1) * 128],
                              cx.hb[k][:], start=(k == 0), stop=(k == ND - 1))
                    em.act(cx.hb[8 + fi8][:], zb[:], AF.Gelu,
                           bias=cx.bcol("b1_3", half * 16 + fi))
                for di in range(8):
                    yb = cx.ps[di % 4]
                    for k in range(8):
                        kk = sub * 8 + k
                        em.mm(yb[:], cx.WC[kk // 2][:, (kk % 2) * 1024 + di * 128:
                                                    (kk % 2) * 1024 + (di + 1) * 128],
                              cx.hb[8 + k][:], start=(k == 0), stop=(k == 7))
                    if half == 0 and sub == 0:
                        em.dma("y", cx.fs[0][:],
                               W["aoscr"][di * 128:(di + 1) * 128, cs])
                        em.stt(cx.ybuf[:, (di % 4) * NT:(di % 4 + 1) * NT], yb[:],
                               cx.bcol("b2_3", di), cx.fs[0][:], ALU.add, ALU.add)
                    else:
                        em.act(cx.ybuf[:, (di % 4) * NT:(di % 4 + 1) * NT], yb[:],
                               AF.Copy)
                    if di % 4 == 3:
                        srcb = cx.ybuf[:]
                        hdst = pre2[(di // 4) * 512:(di // 4 + 1) * 512, cs] \
                            .rearrange("(a p) n -> p a n", p=128)
                        s2 = srcb.rearrange("p (a n) -> p a n", n=NT)
                        if half == 0 and sub == 0:
                            def fdma(e, hdst=hdst, s2=s2):
                                return e.dma_start(hdst, s2)
                        else:
                            def fdma(e, hdst=hdst, s2=s2):
                                return e.dma_start(hdst, s2, accum_op=ALU.add)
                        em.add("g", fdma, reads=[srcb], writes=[pre2[:, :]],
                               dma=True)


def l3_ln2(cx, W, pre2, acc):
    em = cx.em
    NL = 256
    for c in range(T // NL):
        cs = slice(c * NL, (c + 1) * NL)
        for t in range(ND):
            em.dma("y", cx.ybuf[:, t * NL:(t + 1) * NL],
                   pre2[t * 128:(t + 1) * 128, cs])
            em.dma("g", cx.hb[t][:, 0:NL], pre2[t * 128:(t + 1) * 128, cs])
            em.act(cx.hb[8 + t][:, 0:NL], cx.hb[t][:, 0:NL], AF.Square)
        for t in range(ND):
            em.mm(cx.ps[0][0:1, 0:NL], cx.ones_col[:], cx.hb[t][:, 0:NL],
                  start=(t == 0), stop=(t == ND - 1))
        for t in range(ND):
            em.mm(cx.ps[1][0:1, 0:NL], cx.ones_col[:], cx.hb[8 + t][:, 0:NL],
                  start=(t == 0), stop=(t == ND - 1))
        mu = cx.r4[:, 0:NL]
        r0 = cx.r4[:, 256:256 + NL]
        r1 = cx.r4[:, 512:512 + NL]
        rs = cx.r4[:, 768:768 + NL]
        em.act(mu, cx.ps[0][0:1, 0:NL], AF.Copy, scale=1.0 / D)
        em.act(r0, cx.ps[1][0:1, 0:NL], AF.Copy, scale=1.0 / D)
        em.tt(r1, mu, mu, ALU.mult)
        em.tt(r0, r0, r1, ALU.subtract)
        em.act(r1, r0, AF.Ln, bias=cx.bcol("cst", 1)[0:1, :])
        em.act(rs, r1, AF.Exp, scale=-0.5)
        em.vcopy(cx.mu_bf, mu)
        em.vcopy(cx.rs_bf, rs)
        em.mm(cx.ps[2][:, 0:NL], cx.ones_row[:], cx.mu_bf)
        em.mm(cx.ps[3][:, 0:NL], cx.ones_row[:], cx.rs_bf)
        for t in range(ND):
            sl = cx.ybuf[:, t * NL:(t + 1) * NL]
            em.tt(sl, sl, cx.ps[2][:, 0:NL], ALU.subtract)
            em.tt(sl, sl, cx.ps[3][:, 0:NL], ALU.mult)
            em.ts(sl, sl, cx.bcol("g2", t), cx.bcol("be2", t),
                  op0=ALU.mult, op1=ALU.add)
            dst = acc[t * 128:(t + 1) * 128, cs]

            def fdma(e, dst=dst, s=sl):
                return e.dma_start(dst, s, accum_op=ALU.add)
            em.add("g", fdma, reads=[sl], writes=[acc[:, :]], dma=True)


def l3_ffn(cx, acc):
    em = cx.em
    W = cx.W
    for k in range(ND):
        em.dma("g", cx.gsm[k][:, 16:17], W["rt3"][k * 128:(k + 1) * 128, :])
    for i in range(32):
        tok = slice(i * 128, (i + 1) * 128)
        for k in range(ND):
            em.mm(cx.ps[4][0:128, 0:1], cx.xn[k][:, tok], cx.gsm[k][:, 16:17],
                  start=(k == 0), stop=(k == ND - 1))
        em.act(cx.gscr[:, i:i + 1], cx.ps[4][0:128, 0:1], AF.Copy)
    s32 = cx.gscr[:, 0:32]
    LOGK = float(np.log(np.float32(1152.0), dtype=np.float32))
    A1 = float(np.float32(LOGK - np.log(np.float32(4096.0), dtype=np.float32)))
    a_ = cx.r4[:, 0:1]
    nega = cx.r4[:, 1:2]
    M = cx.r4[:, 2:3]
    S_ = cx.r4[:, 3:4]
    t1 = cx.r4[:, 4:5]
    t2 = cx.r4[:, 5:6]
    rowbuf = cx.r4[:, 8:136]
    em.memset(a_, A1)

    def bcast1(srcrow, dstcol):
        em.mm(cx.ps[7][:, 0:1], cx.ones_rf[:], srcrow)
        em.act(dstcol, cx.ps[7][:, 0:1], AF.Copy)

    def cross_reduce(tile, out, op):
        em.reduce(cx.gcol[:, 7:8], tile, op)
        em.trans(cx.ps[7][0:1, 0:128], cx.gcol[:, 7:8], cx.id_f[:])
        em.act(rowbuf, cx.ps[7][0:1, 0:128], AF.Copy)
        em.reduce(out, rowbuf, op)

    mtile = cx.fs[0][:, 0:32]
    etile = cx.fs[0][:, 32:64]
    for it in range(49):
        em.ts(nega, a_, -1.0, None, op0=ALU.mult)
        bcast1(nega, cx.gcol[:, 6:7])
        em.ts(mtile, s32, cx.gcol[:, 6:7], None, op0=ALU.min)
        cross_reduce(mtile, M, ALU.max)
        em.ts(t1, M, -1.0, None, op0=ALU.mult)
        bcast1(t1, cx.gcol[:, 5:6])
        em.act(etile, mtile, AF.Exp, bias=cx.gcol[:, 5:6])
        cross_reduce(etile, S_, ALU.add)
        em.act(t2, S_, AF.Ln)
        em.ts(t1, M, -1.0, LOGK, op0=ALU.mult, op1=ALU.add)
        em.tt(a_, t1, t2, ALU.subtract)
    em.ts(nega, a_, -1.0, None, op0=ALU.mult)
    bcast1(nega, cx.gcol[:, 6:7])
    em.ts(mtile, s32, cx.gcol[:, 6:7], None, op0=ALU.is_ge)
    key = cx.fs[0][:, 64:96]
    em.tt(etile, cx.gsm_tokkey[:], s32, ALU.subtract)
    em.tt(etile, etile, mtile, ALU.mult)
    em.tt(key, s32, etile, ALU.add)
    lo = cx.r4[:, 0:1]
    hi = cx.r4[:, 1:2]
    mid = cx.r4[:, 2:3]
    cnt = cx.r4[:, 3:4]
    cond = cx.r4[:, 4:5]
    tmp = cx.r4[:, 5:6]
    em.memset(lo, -1.0e6)
    em.memset(hi, 2.0e6)
    for it in range(48):
        em.tt(mid, lo, hi, ALU.add)
        em.ts(mid, mid, 0.5, None, op0=ALU.mult)
        bcast1(mid, cx.gcol[:, 6:7])
        em.ts(etile, key, cx.gcol[:, 6:7], None, op0=ALU.is_ge)
        cross_reduce(etile, cnt, ALU.add)
        em.ts(cond, cnt, 1024.0, None, op0=ALU.is_ge)
        em.tt(tmp, mid, lo, ALU.subtract)
        em.tt(tmp, tmp, cond, ALU.mult)
        em.tt(lo, lo, tmp, ALU.add)
        em.tt(tmp, hi, mid, ALU.subtract)
        em.ts(cond, cond, -1.0, 1.0, op0=ALU.mult, op1=ALU.add)
        em.tt(tmp, tmp, cond, ALU.mult)
        em.tt(hi, hi, tmp, ALU.subtract)
    bcast1(lo, cx.gcol[:, 6:7])
    em.ts(cx.selcw[:, 480:512], key, cx.gcol[:, 6:7], None, op0=ALU.is_ge)

    for k in range(ND):
        em.dma("g", cx.WC[k // 4][:, (k % 4) * 512:(k % 4) * 512 + 512],
               W["lw1"][k * 128:(k + 1) * 128, :])
    for k in range(4):
        em.dma("g", cx.WC[2 + k // 2][:, (k % 2) * 1024:(k % 2) * 1024 + 1024],
               W["lw2"][k * 128:(k + 1) * 128, :])

    def rms_tiles(cs, gname, out_tiles):
        for t in range(ND):
            em.act(cx.bcsb[1][:], cx.xn[t][:, cs], AF.Square)
            em.mm(cx.ps[6][0:1, :], cx.ones_col[:], cx.bcsb[1][:],
                  start=(t == 0), stop=(t == ND - 1))
        r0 = cx.r4[:, 0:NT]
        em.act(r0, cx.ps[6][0:1, :], AF.Ln)
        em.act(r0, r0, AF.Exp, scale=-0.5)
        em.ts(r0, r0, 32.0, None, op0=ALU.mult)
        em.vcopy(cx.brow[0], r0)
        em.mm(cx.ps[6][:], cx.ones_row[:], cx.brow[0])
        for t in range(ND):
            em.tt(cx.fs[0][:], cx.xn[t][:, cs], cx.ps[6][:], ALU.mult)
            em.ts(out_tiles[t][:], cx.fs[0][:], cx.bcol(gname, t), None,
                  op0=ALU.mult)

    for c in range(NCH):
        cs = slice(c * NT, (c + 1) * NT)
        rms_tiles(cs, "lgam", cx.hb[0:8])
        for fi in range(4):
            zb = cx.ps[4 + fi % 2]
            for k in range(ND):
                em.mm(zb[:], cx.WC[k // 4][:, (k % 4) * 512 + fi * 128:
                                           (k % 4) * 512 + (fi + 1) * 128],
                      cx.hb[k][:], start=(k == 0), stop=(k == ND - 1))
            em.act(cx.hb[8 + fi][:], zb[:], AF.Gelu, bias=cx.bcol("lb1", fi))
        for di in range(8):
            yb = cx.ps[di % 4]
            for k in range(4):
                em.mm(yb[:], cx.WC[2 + k // 2][:, (k % 2) * 1024 + di * 128:
                                               (k % 2) * 1024 + (di + 1) * 128],
                      cx.hb[8 + k][:], start=(k == 0), stop=(k == 3))
            em.act(cx.ybuf[:, (di % 4) * NT:(di % 4 + 1) * NT], yb[:],
                   AF.Identity, bias=cx.bcol("lb2", di))
            if di % 4 == 3:
                accum_dma(cx, acc, c, di // 4)

    for half in range(2):
        fo = half * 2048
        for k in range(ND):
            em.dma("g", cx.WB[k][:], W["hw1"][k * 128:(k + 1) * 128,
                                              fo:fo + 2048])
        for k in range(16):
            em.dma("g", cx.WC[k // 2][:, (k % 2) * 1024:(k % 2) * 1024 + 1024],
                   W["hw2"][fo + k * 128:fo + (k + 1) * 128, :])
        for c in range(NCH):
            cs = slice(c * NT, (c + 1) * NT)
            cols = [cx.selcw[:, 480 + c * 4 + i:481 + c * 4 + i]
                    for i in range(4)]
            bcast_col(cx, cols, 6, cx.brow[0], cx.bcsb[0])
            rms_tiles(cs, "hgam", cx.hb[0:8])
            for fi in range(16):
                zb = cx.ps[4 + fi % 2]
                for k in range(ND):
                    em.mm(zb[:], cx.WB[k][:, fi * 128:(fi + 1) * 128],
                          cx.hb[k][:], start=(k == 0), stop=(k == ND - 1))
                em.act(cx.hb[8 + fi % 8][:], zb[:], AF.Gelu,
                       bias=cx.bcol("hb1", half * 16 + fi))
                if fi % 8 == 7:
                    sub = fi // 8
                    for di in range(8):
                        yb = cx.ps[di % 4]
                        for k in range(8):
                            kk = sub * 8 + k
                            em.mm(yb[:], cx.WC[kk // 2][:, (kk % 2) * 1024 + di * 128:
                                                        (kk % 2) * 1024 + (di + 1) * 128],
                                  cx.hb[8 + k][:], start=(k == 0),
                                  stop=(k == 7))
                        if half == 0 and sub == 0:
                            em.stt(cx.fs[0][:], yb[:], cx.bcol("hb2", di),
                                   cx.bcsb[0][:], ALU.add, ALU.mult)
                        else:
                            em.tt(cx.fs[0][:], yb[:], cx.bcsb[0][:], ALU.mult)
                        em.vcopy(cx.ybuf[:, (di % 4) * NT:(di % 4 + 1) * NT],
                                 cx.fs[0][:])
                        if di % 4 == 3:
                            accum_dma(cx, acc, c, di // 4)


def emit_block3(cx, xsrc, acc):
    ln_pass(cx, xsrc, acc)
    l3_attn(cx, acc)
    l3_b2(cx, cx.W, cx.W["qkvscr"], cx.W["pre2scr"])
    l3_ln2(cx, cx.W, cx.W["pre2scr"], acc)
    l3_ffn(cx, acc)

